# version 13
# baseline (speedup 1.0000x reference)
import numpy as np
import ml_dtypes
import concourse.bass as bass
import concourse.mybir as mybir
from concourse.bass_utils import run_bass_kernel_spmd

F32 = mybir.dt.float32
BF16 = mybir.dt.bfloat16
AF = mybir.ActivationFunctionType
ALU = mybir.AluOpType
AX = mybir.AxisListType

T = 2048
NS = 16
D = 1024
NEG = -1e30
MC = 1e30
LN_EPS = 1e-5
RMS_EPS = 1e-6

DRAIN_BEFORE_INC = False

ENGINES = ("tensor", "vector", "scalar", "gpsimd", "sync")


class Sched:
    def __init__(self):
        self.ops = {e: [] for e in ENGINES}
        self.nq = {"sync": 0, "gpsimd": 0}
        self.last_w = {}
        self.readers = {}

    def op(self, eng, fn, r=(), w=()):
        idx = len(self.ops[eng])
        ident = ("e", eng, idx)
        deps, raw = self._deps(r, w, ident)
        self.ops[eng].append(dict(fn=fn, deps=deps, raw=raw, inc=False, dma=None))
        return ident

    def dma(self, eng, fn, r=(), w=()):
        qidx = self.nq[eng]
        self.nq[eng] += 1
        ident = ("q", eng, qidx)
        deps, raw = self._deps(r, w, ident)
        self.ops[eng].append(dict(fn=fn, deps=deps, raw=raw, inc=False, dma=qidx))
        return ident

    def _deps(self, r, w, ident):
        deps = []
        raw = []
        for k in r:
            lw = self.last_w.get(k)
            if lw is not None:
                deps.append(lw)
                raw.append(lw)
            self.readers.setdefault(k, []).append(ident)
        for k in w:
            lw = self.last_w.get(k)
            if lw is not None:
                deps.append(lw)
            deps.extend(self.readers.get(k, ()))
            self.last_w[k] = ident
            self.readers[k] = []
        return deps, raw

    def finalize(self, nc, ctx):
        self.esem = {e: ctx.enter_context(nc.semaphore(f"sem_{e}")) for e in ENGINES}
        self.qsem = {e: ctx.enter_context(nc.semaphore(f"qsem_{e}")) for e in self.nq}
        for eng in ENGINES:
            for o in self.ops[eng]:
                for kind, seng, sidx in o["deps"]:
                    if kind == "e" and seng != eng:
                        self.ops[seng][sidx]["inc"] = True
        pref = {}
        for eng in ENGINES:
            c = 0
            pl = []
            for o in self.ops[eng]:
                if o["inc"]:
                    c += 1
                pl.append(c)
            pref[eng] = pl
        for eng in ENGINES:
            seen = {}
            for o in self.ops[eng]:
                waits = {}
                for kind, seng, sidx in o["deps"]:
                    if kind == "e":
                        if seng == eng:
                            continue
                        sem, val = ("e", seng), pref[seng][sidx]
                        if val == 0:
                            continue
                    else:
                        if seng == eng and o["dma"] is not None:
                            continue
                        sem, val = ("q", seng), 16 * (sidx + 1)
                    if waits.get(sem, 0) < val:
                        waits[sem] = val
                ww = []
                for sem, val in sorted(waits.items(), key=str):
                    if seen.get(sem, 0) >= val:
                        continue
                    seen[sem] = val
                    ww.append((sem, val))
                o["waits"] = ww

    def emit_engine(self, eng, engobj):
        drained_upto = -1
        for i, o in enumerate(self.ops[eng]):
            if eng == "vector":
                same = [sidx for kind, seng, sidx in o["raw"]
                        if kind == "e" and seng == eng]
                if same and max(same) > drained_upto:
                    engobj.drain()
                    drained_upto = i - 1
            for (sk, sname), val in o["waits"]:
                sem = self.esem[sname] if sk == "e" else self.qsem[sname]
                engobj.wait_ge(sem, val)
            ins = o["fn"]()
            if o["dma"] is not None:
                ins.then_inc(self.qsem[eng], 16)
            elif o["inc"]:
                if eng == "vector" and DRAIN_BEFORE_INC:
                    ins = engobj.drain()
                    drained_upto = i
                ins.then_inc(self.esem[eng], 1)

    def final_wait(self, eng, engobj):
        if self.nq.get(eng, 0):
            engobj.wait_ge(self.qsem[eng], 16 * self.nq[eng])


def build_nc(debug=()):
    nc = bass.Bass("TRN2", target_bir_lowering=False)
    S = Sched()

    X = nc.dram_tensor("x", [T, D], F32, kind="ExternalInput").ap()
    dWq = nc.dram_tensor("Wq", [D, D], F32, kind="ExternalInput").ap()
    dbq = nc.dram_tensor("bq", [1, D], F32, kind="ExternalInput").ap()
    dWdown = nc.dram_tensor("Wdown", [D, 64], F32, kind="ExternalInput").ap()
    dbdown = nc.dram_tensor("bdown", [1, 64], F32, kind="ExternalInput").ap()
    dWup = nc.dram_tensor("Wup", [64, 2 * D], F32, kind="ExternalInput").ap()
    dbup = nc.dram_tensor("bup", [1, 2 * D], F32, kind="ExternalInput").ap()
    dWo = nc.dram_tensor("Wo", [D, D], F32, kind="ExternalInput").ap()
    dbo = nc.dram_tensor("bo", [1, D], F32, kind="ExternalInput").ap()
    dWg = nc.dram_tensor("Wg", [D, 8], F32, kind="ExternalInput").ap()
    dbg = nc.dram_tensor("bg", [1, 8], F32, kind="ExternalInput").ap()
    dWe = nc.dram_tensor("We", [8, D, D], F32, kind="ExternalInput").ap()
    dg1 = nc.dram_tensor("gamma1", [1, D], F32, kind="ExternalInput").ap()
    dg2 = nc.dram_tensor("gamma2", [1, D], F32, kind="ExternalInput").ap()
    dWr = nc.dram_tensor("Wr", [D, D], F32, kind="ExternalInput").ap()
    dWf = nc.dram_tensor("Wf", [D, D], F32, kind="ExternalInput").ap()
    dattr = nc.dram_tensor("att_r", [1, 128], F32, kind="ExternalInput").ap()
    dlng = nc.dram_tensor("ln_gamma", [1, D], F32, kind="ExternalInput").ap()
    dlnb = nc.dram_tensor("ln_beta", [1, D], F32, kind="ExternalInput").ap()
    FRS = nc.dram_tensor("fr_scratch", [128, NS * 1024], BF16).ap()
    OUT = nc.dram_tensor("out", [T, D], F32, kind="ExternalOutput").ap()
    dbgout = {}
    for name in debug:
        shape = {"u": [128, D], "h": [128, D], "q": [128, 2048], "k": [128, 2048],
                 "v": [128, 2048], "outs": [128, 2048], "z": [128, D],
                 "x1": [128, D], "x2": [128, D], "gate": [128, 64],
                 "u2": [128, D], "sr": [8, T], "p": [8, T],
                 "ctx": [128, 8], "fh": [128, D], "ofmp": [128, D],
                 "wup": [128, 2048], "wdn": [128, 512], "wq": [128, 8 * D],
                 "bupk": [1, 2048], "bupv": [1, 2048], "wvp": [128, 2048],
                 "d0": [128, 128], "wdnf": [128, 512], "zfm": [128, 8192]}[name]
        dbgout[name] = nc.dram_tensor("dbg_" + name, shape, F32, kind="ExternalOutput").ap()

    from contextlib import ExitStack
    ctx = ExitStack()
    with ctx:
        sbt = lambda n, s, d=F32: ctx.enter_context(nc.sbuf_tensor(n, s, d))
        x_res = sbt("x_res", [128, NS * D])
        Wq_sb = sbt("Wq_sb", [128, 8 * D], BF16)
        Wo_sb = sbt("Wo_sb", [128, 8 * D], BF16)
        We_sb = sbt("We_sb", [128, 8192], BF16)
        Wdown_sb = sbt("Wdown_sb", [128, 8 * 64], BF16)
        Wup_sb = sbt("Wup_sb", [128, 2048], BF16)
        Wup_vpad = sbt("Wup_vpad", [128, 2048], BF16)
        Wg_sb = sbt("Wg_sb", [128, 64], BF16)
        g1_m = sbt("g1_m", [128, D], BF16)
        g2_m = sbt("g2_m", [128, D], BF16)
        bq_rows = sbt("bq_rows", [1, 1024], BF16)
        bdown_row = sbt("bdown_row", [1, 64], BF16)
        bupk_rows = sbt("bupk_rows", [1, 2048], BF16)
        bupv_rows = sbt("bupv_rows", [1, 2048], BF16)
        bo_hi = sbt("bo_hi", [1, D], BF16)
        bo_lo = sbt("bo_lo", [1, D], BF16)
        bg_hi = sbt("bg_hi", [1, 8], BF16)
        bg_lo = sbt("bg_lo", [1, 8], BF16)
        row_f = sbt("row_f", [1, 512])
        row_hi = sbt("row_hi", [1, D], BF16)
        ones_col = sbt("ones_col", [1, 128], BF16)
        ident = sbt("ident", [128, 128], BF16)
        shiftdn = sbt("shiftdn", [128, 128], BF16)
        identbot = sbt("identbot", [128, 128], BF16)
        m9l = sbt("m9l", [9, 128], BF16)
        m9r = sbt("m9r", [9, 512], BF16)
        u_ring = sbt("u_ring", [128, 2 * D])
        h_tok = sbt("h_tok", [128, 2 * D], BF16)
        h_fm = sbt("h_fm", [128, 2 * D], BF16)
        zfm = sbt("zfm", [128, 8 * D], BF16)
        q_strip = sbt("q_strip", [128, 2048], BF16)
        k_strip = sbt("k_strip", [128, 2048], BF16)
        v_strip = sbt("v_strip", [128, 2048], BF16)
        outSraw = sbt("outSraw", [128, 2048], BF16)
        ofmp = sbt("ofmp", [128, D], BF16)
        dumpA = sbt("dumpA", [128, D], BF16)
        dumpB = sbt("dumpB", [128, D], BF16)
        dumpC = sbt("dumpC", [128, D], BF16)
        pv_buf = sbt("pv_buf", [128, 2 * D], BF16)
        sr_all = pv_buf[0:8, 0:T]
        stats = sbt("stats", [128, 64])
        gw = sbt("gw", [128, 64])
        glog = sbt("glog", [128, 8])
        pT = sbt("pT", [128, 128], BF16)
        ctx_bf = sbt("ctx_bf", [128, 8], BF16)
        fhFM_bf = sbt("fhFM_bf", [128, 8], BF16)
        fhT8 = sbt("fhT8", [8, 128], BF16)
        srtok_bf = sbt("srtok_bf", [128, 8], BF16)
        srtok_f = sbt("srtok_f", [128, 8])
        eps_ln = sbt("eps_ln", [128, 1])
        eps_rms = sbt("eps_rms", [128, 1])

        psQ1 = ctx.enter_context(nc.psum_tensor("psQ1", [128, 512], F32))
        psQ2 = ctx.enter_context(nc.psum_tensor("psQ2", [128, 512], F32))
        psK1 = ctx.enter_context(nc.psum_tensor("psK1", [128, 512], F32))
        psK2 = ctx.enter_context(nc.psum_tensor("psK2", [128, 512], F32))
        psS = ctx.enter_context(nc.psum_tensor("psS", [128, 512], F32))
        psO = ctx.enter_context(nc.psum_tensor("psO", [128, 512], F32))
        psT1 = ctx.enter_context(nc.psum_tensor("psT1", [128, 1024], BF16))
        psT2 = ctx.enter_context(nc.psum_tensor("psT2", [128, 1024], BF16))

        lng_m = q_strip[:].bitcast(F32)
        lnb_m = k_strip[:].bitcast(F32)
        fh_m = outSraw[:, 0:1024]
        attr_m = outSraw[:, 1024:2048]

        V, SC, TE, G, SY = nc.vector, nc.scalar, nc.tensor, nc.gpsimd, nc.sync

        stat_alloc = [0]

        def stcols(n):
            c = stat_alloc[0]
            stat_alloc[0] += n
            assert stat_alloc[0] <= 64, "stats overflow"
            return c

        def st(c):
            return stats[:, c:c + 1]

        def stk(c):
            return f"stc{c}"

        def dbg_store(name, src_ap, rkeys):
            if name in dbgout:
                S.dma("gpsimd", lambda: G.dma_start(dbgout[name], src_ap), r=list(rkeys))

        def bcast_rows(dst_sb, dst_key, src_dram, n=D):
            for hh in range(0, n, 512):
                nn = min(512, n - hh)
                S.dma("sync", lambda hh=hh, nn=nn: SY.dma_start(row_f[:, 0:nn],
                      src_dram[:, hh:hh + nn]), w=["row_f"])
                S.op("vector", lambda hh=hh, nn=nn: V.tensor_copy(row_hi[:, hh:hh + nn],
                     row_f[:, 0:nn]), r=["row_f"], w=["row_hi"])
                S.op("tensor", lambda hh=hh, nn=nn: TE.matmul(psS[:, 0:nn], ones_col[:],
                     row_hi[:, hh:hh + nn], start=True, stop=False),
                     r=["row_hi", "ones_col"], w=["psS"])
                S.op("vector", lambda hh=hh, nn=nn: V.tensor_tensor(row_f[:, 0:nn], row_f[:, 0:nn],
                     row_hi[:, hh:hh + nn], ALU.subtract), r=["row_hi"], w=["row_f"])
                S.op("vector", lambda hh=hh, nn=nn: V.tensor_copy(row_hi[:, hh:hh + nn],
                     row_f[:, 0:nn]), r=["row_f"], w=["row_hi"])
                S.op("tensor", lambda hh=hh, nn=nn: TE.matmul(psS[:, 0:nn], ones_col[:],
                     row_hi[:, hh:hh + nn], start=False, stop=True),
                     r=["row_hi", "ones_col"], w=["psS"])
                S.op("vector", lambda hh=hh, nn=nn: V.tensor_copy(dst_sb[:, hh:hh + nn], psS[:, 0:nn]),
                     r=["psS"], w=[dst_key])

        def bcast_bf16(dst_sb, dst_key, src_dram, n=D):
            for hh in range(0, n, 512):
                nn = min(512, n - hh)
                S.dma("sync", lambda hh=hh, nn=nn: SY.dma_start(row_f[:, 0:nn],
                      src_dram[:, hh:hh + nn]), w=["row_f"])
                S.op("vector", lambda hh=hh, nn=nn: V.tensor_copy(row_hi[:, hh:hh + nn],
                     row_f[:, 0:nn]), r=["row_f"], w=["row_hi"])
            for hh in range(0, n, 512):
                nn = min(512, n - hh)
                S.op("tensor", lambda hh=hh, nn=nn: TE.matmul(psS[:, 0:nn], ones_col[:],
                     row_hi[:, hh:hh + nn], start=True, stop=True),
                     r=["row_hi", "ones_col"], w=["psS"])
                S.op("vector", lambda hh=hh, nn=nn: V.tensor_copy(dst_sb[:, hh:hh + nn], psS[:, 0:nn]),
                     r=["psS"], w=[dst_key])

        def hilo_row(dst_hi, dst_lo, khi, klo, src_dram, n):
            for hh in range(0, n, 512):
                nn = min(512, n - hh)
                S.dma("sync", lambda hh=hh, nn=nn: SY.dma_start(row_f[:, 0:nn],
                      src_dram[:, hh:hh + nn]), w=["row_f"])
                S.op("vector", lambda hh=hh, nn=nn: V.tensor_copy(dst_hi[:, hh:hh + nn],
                     row_f[:, 0:nn]), r=["row_f"], w=[khi])
                S.op("vector", lambda hh=hh, nn=nn: V.tensor_tensor(row_f[:, 0:nn], row_f[:, 0:nn],
                     dst_hi[:, hh:hh + nn], ALU.subtract), r=[khi], w=["row_f"])
                S.op("vector", lambda hh=hh, nn=nn: V.tensor_copy(dst_lo[:, hh:hh + nn],
                     row_f[:, 0:nn]), r=["row_f"], w=[klo])

        Xv = X.rearrange("(s p) f -> p s f", p=128)
        S.dma("sync", lambda: SY.dma_start(x_res[:].rearrange("p (s f) -> p s f", s=NS), Xv),
              w=[f"x{s}" for s in range(NS)])

        def g_masks():
            G.memset(ident[:], 0.0)
            G.affine_select(out=ident[:], in_=ident[:], compare_op=ALU.not_equal,
                            fill=1.0, base=0, pattern=[[-1, 128]], channel_multiplier=1)
            G.memset(shiftdn[:], 0.0)
            G.affine_select(out=shiftdn[:], in_=shiftdn[:], compare_op=ALU.not_equal,
                            fill=1.0, base=-64, pattern=[[-1, 128]], channel_multiplier=1)
            G.memset(identbot[:], 0.0)
            G.affine_select(out=identbot[:], in_=identbot[:], compare_op=ALU.not_equal,
                            fill=1.0, base=0, pattern=[[-1, 128]], channel_multiplier=1)
            G.memset(identbot[0:64, :], 0.0)
            G.memset(m9r[:], 0.0)
            G.affine_select(out=m9r[:], in_=m9r[:], compare_op=ALU.not_equal,
                            fill=1.0, base=-1, pattern=[[0, 4], [-1, 8], [0, 16]],
                            channel_multiplier=1)
            G.memset(m9r[0:1, :], 1.0)
            G.memset(m9l[:], 0.0)
            G.affine_select(out=m9l[:], in_=m9l[:], compare_op=ALU.not_equal,
                            fill=MC, base=-1, pattern=[[-1, 8], [0, 16]],
                            channel_multiplier=1)
            return G.memset(m9l[0:1, :], -MC)
        S.op("gpsimd", g_masks, w=["ident", "shiftdn", "identbot", "m9"])
        S.op("vector", lambda: V.memset(ones_col[:], 1.0), w=["ones_col"])
        S.op("vector", lambda: V.memset(eps_ln[:], LN_EPS), w=["eps_ln"])
        S.op("vector", lambda: V.memset(eps_rms[:], RMS_EPS), w=["eps_rms"])
        def s_warm():
            for f in (AF.Identity, AF.Copy, AF.Square, AF.Exp, AF.Ln, AF.Lrelu):
                kw = dict(alpha=0.01) if f == AF.Lrelu else {}
                SC.activation(stats[0:1, 0:1], eps_ln[0:1, 0:1], f, **kw)
            return SC.copy(stats[0:1, 0:1], eps_ln[0:1, 0:1])
        S.op("scalar", s_warm, r=["eps_ln"], w=["warmup"])
        def v_warm():
            c0, c1 = stats[0:1, 2:3], stats[0:1, 3:4]
            V.tensor_copy(c0, eps_ln[0:1, 0:1])
            V.tensor_tensor(c1, c0, c0, ALU.add)
            V.tensor_tensor(c1, c0, c0, ALU.mult)
            V.tensor_tensor(c1, c0, c0, ALU.subtract)
            V.tensor_tensor(c1, c0, c0, ALU.max)
            V.tensor_scalar(c1, c0, 1.0, 2.0, ALU.mult, ALU.add)
            V.tensor_scalar(c1, c0, c0, c0, ALU.subtract, ALU.mult)
            V.tensor_reduce(c1, stats[0:1, 0:2], AX.X, ALU.add)
            V.tensor_reduce(c1, stats[0:1, 0:2], AX.X, ALU.max, negate=True)
            V.reciprocal(c1, c0)
            return V.scalar_tensor_tensor(c1, c0, 1.0, c0, ALU.mult, ALU.add)
        S.op("vector", v_warm, r=["eps_ln"], w=["vwarmup"])
        S.op("vector", lambda: V.memset(q_strip[:], 0.0), w=["q_strip"])
        S.op("vector", lambda: V.memset(v_strip[0:64, :], 0.0), w=["v_strip_z"])
        S.op("vector", lambda: V.memset(Wup_vpad[:], 0.0), w=["Wup_vpad"])
        S.op("vector", lambda: V.memset(dumpB[:], 0.0), w=["dumpB"])

        bcast_bf16(g1_m, "g1_m", dg1, D)
        bcast_bf16(g2_m, "g2_m", dg2, D)
        S.dma("gpsimd", lambda: G.dma_start(bq_rows[:], dbq), w=["bq_rows"])
        S.dma("gpsimd", lambda: G.dma_start(bdown_row[:], dbdown), w=["bdown_row"])
        bup3 = dbup.rearrange("o (h v d) -> o h v d", h=16, v=2, d=64)
        bupk3 = bupk_rows[:].rearrange("o (h v d) -> o h v d", h=16, v=2, d=64)
        bupv3 = bupv_rows[:].rearrange("o (h v d) -> o h v d", h=16, v=2, d=64)
        S.dma("gpsimd", lambda: G.dma_start(bupk3[:, :, 0, :], bup3[:, :, 0, :]), w=["bupk"])
        S.dma("gpsimd", lambda: G.dma_start(bupk3[:, :, 1, :], bup3[:, :, 0, :]), w=["bupk"])
        S.op("vector", lambda: V.memset(bupv_rows[:], 0.0), w=["bupv"])
        S.dma("gpsimd", lambda: G.dma_start(bupv3[:, :, 1, :], bup3[:, :, 1, :]), w=["bupv"])
        hilo_row(bo_hi, bo_lo, "bo_hi", "bo_lo", dbo, D)
        hilo_row(bg_hi, bg_lo, "bg_hi", "bg_lo", dbg, 8)

        def load_ffn_weights():
            S.dma("gpsimd", lambda: G.dma_start(Wq_sb[:].rearrange("p (c f) -> p c f", c=8), dWq.rearrange("(c p) f -> p c f", p=128)),
                  w=["Wq_sb"])
            S.dma("gpsimd", lambda: G.dma_start(Wdown_sb[:].rearrange("p (c f) -> p c f", c=8),
                  dWdown.rearrange("(c p) f -> p c f", p=128)), w=["Wdown_sb"])
            S.op("vector", lambda: V.memset(Wup_sb[64:128, :], 0.0), w=["Wup_sb"])
            kvsrc = dWup.rearrange("p (h v d) -> p h v d", h=16, v=2, d=64)
            for rep in range(2):
                dst = Wup_sb[0:64, 0:2048].rearrange("p (h v d) -> p h v d", h=16, v=2, d=64)
                S.dma("gpsimd", lambda dst=dst, rep=rep: G.dma_start(dst[:, :, rep, :], kvsrc[:, :, 0, :]),
                      w=["Wup_sb"])
            dstv = Wup_vpad[0:64, :].rearrange("p (h v d) -> p h v d", h=16, v=2, d=64)
            S.dma("gpsimd", lambda dstv=dstv: G.dma_start(dstv[:, :, 1, :], kvsrc[:, :, 1, :]),
                  w=["Wup_vpad"])
            S.dma("gpsimd", lambda: G.dma_start(Wo_sb[:].rearrange("p (c f) -> p c f", c=8), dWo.rearrange("(c p) f -> p c f", p=128)),
                  w=["Wo_sb"])
            S.dma("gpsimd", lambda: G.dma_start(Wg_sb[:].rearrange("p (c f) -> p c f", c=8), dWg.rearrange("(c p) f -> p c f", p=128)),
                  w=["Wg_sb"])

        cL_sum = [stcols(1), stcols(1)]; cL_mu = [stcols(1), stcols(1)]
        cL_v = [stcols(1), stcols(1)]; cL_t = [stcols(1), stcols(1)]
        cL_r = [stcols(1), stcols(1)]; c_r1 = [stcols(1), stcols(1)]
        c_r3 = [stcols(1), stcols(1)]; c_r3h = [stcols(1), stcols(1)]
        c_zg = [stcols(1), stcols(1)]; c_gt = [stcols(1), stcols(1)]
        c_mx = [stcols(4), stcols(4)]; c_z4 = [stcols(4), stcols(4)]
        c_rz = [stcols(4), stcols(4)]
        c_ymu = [stcols(1), stcols(1)]; c_yv = [stcols(1), stcols(1)]
        c_yr = [stcols(1), stcols(1)]
        c_pm = stcols(1); c_pz = stcols(1)
        c_nmr = [stcols(1), stcols(1)]

        def emit_lnstats(kx, xs, c_sumv, c_muv, c_vv, c_tv, c_rv, eps, dump=None):
            eps = (eps[0], eps[1]) if isinstance(eps, tuple) else eps
            """LayerNorm stats for slab src xs (key kx) -> NEGATIVE mu in c_muv, r in c_rv."""
            db, kdb = dump if dump is not None else (dumpA, "dumpA")
            S.op("vector", lambda: V.tensor_reduce(st(c_sumv), xs, AX.X, ALU.add),
                 r=[kx], w=[stk(c_sumv)])
            S.op("scalar", lambda: SC.activation(db[:], xs, AF.Square, accum_out=st(c_vv)),
                 r=[kx], w=[kdb, stk(c_vv)])
            S.op("scalar", lambda: SC.activation(st(c_muv), st(c_sumv), AF.Identity, scale=-1.0 / D),
                 r=[stk(c_sumv)], w=[stk(c_muv)])
            S.op("scalar", lambda: SC.activation(st(c_tv), st(c_muv), AF.Square),
                 r=[stk(c_muv)], w=[stk(c_tv)])
            S.op("vector", lambda: V.tensor_scalar(st(c_vv), st(c_vv), 1.0 / D, st(c_tv),
                 ALU.mult, ALU.subtract), r=[stk(c_vv), stk(c_tv)], w=[stk(c_vv)])
            S.op("scalar", lambda: SC.activation(st(c_vv), st(c_vv), AF.Ln, bias=eps[0][:]),
                 r=[stk(c_vv), eps[1]], w=[stk(c_vv)])
            S.op("scalar", lambda: SC.activation(st(c_rv), st(c_vv), AF.Exp, scale=-0.5),
                 r=[stk(c_vv)], w=[stk(c_rv)])

        def emit_rms(ksrc, src, c_rv, scale_after=None, dump=None):
            db, kdb = dump if dump is not None else (dumpA, "dumpA")
            S.op("scalar", lambda: SC.activation(db[:], src, AF.Square, accum_out=st(c_rv)),
                 r=[ksrc], w=[kdb, stk(c_rv)])
            S.op("scalar", lambda: SC.activation(st(c_rv), st(c_rv), AF.Ln, bias=eps_rms[:], scale=1.0 / D),
                 r=[stk(c_rv), "eps_rms"], w=[stk(c_rv)])
            S.op("scalar", lambda: SC.activation(st(c_rv), st(c_rv), AF.Exp, scale=-0.5),
                 r=[stk(c_rv)], w=[stk(c_rv)])

        def emit_ln_u(s, kx, xs, ku, us):
            sp = s % 2
            lsum, lmu, lv, lt, lr = (cL_sum[sp], cL_mu[sp], cL_v[sp], cL_t[sp], cL_r[sp])
            nmr = c_nmr[sp]
            emit_lnstats(kx, xs, lsum, lmu, lv, lt, lr, (eps_ln, 'eps_ln'),
                         dump=(dumpC, "dumpC"))
            S.op("vector", lambda lmu=lmu, lr=lr, nmr=nmr: V.tensor_scalar(
                st(nmr), st(lmu), 1.0, st(lr), ALU.mult, ALU.mult),
                r=[stk(lmu), stk(lr)], w=[stk(nmr)])
            S.op("scalar", lambda xs=xs, us=us, lr=lr, nmr=nmr: SC.activation(
                us, xs, AF.Identity, scale=st(lr), bias=st(nmr)),
                r=[kx, stk(lr), stk(nmr)], w=[ku])

        def emit_slab_head(s, fid, dbg_on):
            xs = x_res[:, D * s:D * s + D]
            uslot = s % 2
            sp = s % 2
            us = u_ring[:, D * uslot:D * uslot + D]
            kx, ku = f"x{s}", f"u{uslot}"
            r1c = c_r1[sp]
            emit_ln_u(s, kx, xs, ku, us)
            if dbg_on and s == 0:
                dbg_store("u", u_ring[:, 0:D], [ku])
            emit_rms(ku, us, r1c, dump=(dumpC, "dumpC"))
            hs = h_tok[:, D * (s % 2):D * (s % 2) + D]
            kh = f"h{s % 2}"
            S.op("scalar", lambda us=us, r1c=r1c: SC.activation(
                dumpC[:], us, AF.Identity, scale=st(r1c)), r=[ku, stk(r1c)], w=["dumpC"])
            S.op("vector", lambda hs=hs: V.tensor_tensor(hs, dumpC[:], g1_m[:], ALU.mult),
                 r=["dumpC", "g1_m"], w=[kh])
            if dbg_on and s == 0:
                dbg_store("h", h_tok[:, 0:D], [kh])

        def emit_slab_body(s, fid, dbg_on, feed=None):
            xs = x_res[:, D * s:D * s + D]
            uslot = s % 2
            sp = s % 2
            us = u_ring[:, D * uslot:D * uslot + D]
            kx, ku = f"x{s}", f"u{uslot}"
            r3c, r3hc, zgc, gtc = (c_r3[sp], c_r3h[sp], c_zg[sp], c_gt[sp])
            mxc, z4c, rzc = c_mx[sp], c_z4[sp], c_rz[sp]
            hs = h_tok[:, D * (s % 2):D * (s % 2) + D]
            kh = f"h{s % 2}"
            hfm = h_fm[:, D * (s % 2):D * (s % 2) + D]
            khf = f"hfm{s % 2}"
            for c in range(8):
                S.op("tensor", lambda hs=hs, c=c: TE.transpose(
                    psT2[:, 128 * c:128 * c + 128], hs[:, 128 * c:128 * c + 128], ident[:]),
                    r=[kh, "ident"], w=["psT2.0", "psT2.1"])
            S.op("vector", lambda hfm=hfm: V.tensor_copy(hfm, psT2[:]),
                 r=["psT2.0", "psT2.1"], w=[khf])
            for b, ps, kps in ((0, psQ1, "psQ1"), (1, psQ2, "psQ2")):
                for oc in range(4):
                    occ = 4 * b + oc
                    S.op("tensor", lambda ps=ps, oc=oc, occ=occ: TE.matmul(
                        ps[:, 128 * oc:128 * oc + 128], bq_rows[:, 128 * occ:128 * occ + 128],
                        ones_col[:], start=True, stop=False),
                        r=["bq_rows", "ones_col"], w=[kps])
                    for kc in range(8):
                        S.op("tensor", lambda ps=ps, oc=oc, occ=occ, kc=kc, hfm=hfm: TE.matmul(
                            ps[:, 128 * oc:128 * oc + 128],
                            Wq_sb[:, D * kc + 128 * occ:D * kc + 128 * occ + 128],
                            hfm[:, 128 * kc:128 * kc + 128], start=False, stop=(kc == 7)),
                            r=["Wq_sb", khf], w=[kps])
            S.op("tensor", lambda: TE.matmul(psS[0:64, 0:128], bdown_row[:],
                 ones_col[:], start=True, stop=False), r=["bdown_row", "ones_col"], w=["psS"])
            for kc in range(8):
                S.op("tensor", lambda kc=kc, hfm=hfm: TE.matmul(psS[0:64, 0:128],
                     Wdown_sb[:, 64 * kc:64 * kc + 64], hfm[:, 128 * kc:128 * kc + 128],
                     start=False, stop=(kc == 7)), r=["Wdown_sb", khf], w=["psS"])
            S.op("scalar", lambda: SC.copy(dumpB[0:64, 0:128], psS[0:64, 0:128]),
                 r=["psS"], w=["dumpB"])
            if dbg_on and s == 0:
                dbg_store("d0", dumpB[:, 0:128], ["dumpB"])
            kvv = k_strip[:].rearrange("p (t h) -> p h t", t=128, h=16)
            vvv = v_strip[:].rearrange("p (t h) -> p h t", t=128, h=16)
            for grp in range(4):
                ps, kps = (psK1, "psK1") if grp % 2 == 0 else (psK2, "psK2")
                for j in range(4):
                    h = 4 * grp + j
                    S.op("tensor", lambda ps=ps, j=j, h=h: TE.matmul(
                        ps[:, 128 * j:128 * j + 128], bupk_rows[:, 128 * h:128 * h + 128],
                        ones_col[:], start=True, stop=False), r=["bupk", "ones_col"], w=[kps])
                    S.op("tensor", lambda ps=ps, j=j, h=h: TE.matmul(
                        ps[:, 128 * j:128 * j + 128], Wup_sb[:, 128 * h:128 * h + 128],
                        dumpB[:, 0:128], start=False, stop=True),
                        r=["Wup_sb", "dumpB"], w=[kps])
                src = ps[:].rearrange("p (c t) -> p c t", c=4, t=128)
                for j in range(4):
                    S.op("vector", lambda src=src, grp=grp, j=j, kvv=kvv: V.tensor_copy(
                        kvv[:, 4 * grp + j, :], src[:, j, :]), r=[kps], w=["k_strip"])
            for grp in range(4):
                ps, kps = (psK1, "psK1") if grp % 2 == 0 else (psK2, "psK2")
                for j in range(4):
                    h = 4 * grp + j
                    S.op("tensor", lambda ps=ps, j=j, h=h: TE.matmul(
                        ps[:, 128 * j:128 * j + 128], bupv_rows[:, 128 * h:128 * h + 128],
                        ones_col[:], start=True, stop=False), r=["bupv", "ones_col"], w=[kps])
                    S.op("tensor", lambda ps=ps, j=j, h=h: TE.matmul(
                        ps[:, 128 * j:128 * j + 128], Wup_vpad[:, 128 * h:128 * h + 128],
                        dumpB[:, 0:128], start=False, stop=True),
                        r=["Wup_vpad", "dumpB"], w=[kps])
                src = ps[64:128, :].rearrange("p (c t) -> p c t", c=4, t=128)
                for j in range(4):
                    S.op("vector", lambda src=src, grp=grp, j=j, vvv=vvv: V.tensor_copy(
                        vvv[64:128, 4 * grp + j, :], src[:, j, :]), r=[kps], w=["v_strip"])
            qv = q_strip[:].rearrange("p (t h) -> p h t", t=128, h=16)
            for b, ps, kps in ((0, psQ1, "psQ1"), (1, psQ2, "psQ2")):
                src = ps[:].rearrange("p (c t) -> p c t", c=4, t=128)
                for oc in range(4):
                    occ = 4 * b + oc
                    S.op("scalar", lambda src=src, oc=oc, occ=occ, qv=qv: SC.copy(
                        qv[0:64, 2 * occ, :], src[0:64, oc, :]), r=[kps], w=["q_strip"])
                    S.op("scalar", lambda src=src, oc=oc, occ=occ, qv=qv: SC.copy(
                        qv[64:128, 2 * occ + 1, :], src[64:128, oc, :]), r=[kps], w=["q_strip"])
            if dbg_on and s == 0:
                dbg_store("q", q_strip[:], ["q_strip"])
                dbg_store("k", k_strip[:], ["k_strip"])
                dbg_store("v", v_strip[:], ["v_strip", "v_strip_z"])
            def blk_env(blk):
                bp = blk % 2
                return dict(
                    bp=bp,
                    pS=(psS if bp == 0 else psK1), kS=("psS" if bp == 0 else "psK1"),
                    pO=(psO if bp == 0 else psK2), kO=("psO" if bp == 0 else "psK2"),
                    Ebuf=(dumpA[:, 512:1024] if bp == 0 else dumpB[:, 512:1024]),
                    kE=("dumpA" if bp == 0 else "dumpB"),
                    Pbuf=pv_buf[:, 1024 * bp:1024 * bp + 512],
                    Vbuf=pv_buf[:, 1024 * bp + 512:1024 * bp + 1024],
                    kPV="pv%d" % bp, tcol=512 * bp,
                    kT1=f"psT1.{bp}", kT2=f"psT2.{bp}")

            def blk_A(blk):
                E = blk_env(blk)
                pS, kS, Ebuf, kE, Pbuf, kPV = E["pS"], E["kS"], E["Ebuf"], E["kE"], E["Pbuf"], E["kPV"]
                S.op("tensor", lambda pS=pS: TE.matmul(pS[:], m9l[:], m9r[:],
                     start=True, stop=False), r=["m9"], w=[kS])
                for j in range(4):
                    g = 4 * blk + j
                    S.op("tensor", lambda pS=pS, g=g, j=j: TE.matmul(
                        pS[:, 128 * j:128 * j + 128], q_strip[:, 128 * g:128 * g + 128],
                        k_strip[:, 128 * g:128 * g + 128], start=False, stop=True),
                        r=["q_strip", "k_strip"], w=[kS])
                S.op("vector", lambda pS=pS, mxc=mxc: V.tensor_reduce(
                    stats[:, mxc:mxc + 4], pS[:].rearrange("p (j t) -> p j t", j=4),
                    AX.X, ALU.max, negate=True), r=[kS], w=[stk(mxc)])
                S.op("scalar", lambda mxc=mxc: SC.activation(
                    stats[:, mxc:mxc + 4], stats[:, mxc:mxc + 4], AF.Identity, scale=0.125),
                    r=[stk(mxc)], w=[stk(mxc)])
                for j in range(4):
                    S.op("scalar", lambda pS=pS, Ebuf=Ebuf, j=j, mxc=mxc, z4c=z4c: SC.activation(
                        Ebuf[:, 128 * j:128 * j + 128], pS[:, 128 * j:128 * j + 128],
                        AF.Exp, scale=0.125, bias=st(mxc + j), accum_out=st(z4c + j)),
                        r=[kS, stk(mxc)], w=[kE, stk(z4c)])
                S.op("vector", lambda rzc=rzc, z4c=z4c: V.reciprocal(stats[:, rzc:rzc + 4],
                     stats[:, z4c:z4c + 4]), r=[stk(z4c)], w=[stk(rzc)])
                for j in range(4):
                    S.op("scalar", lambda Pbuf=Pbuf, Ebuf=Ebuf, j=j, rzc=rzc, kE=kE, kPV=kPV: SC.activation(
                        Pbuf[:, 128 * j:128 * j + 128], Ebuf[:, 128 * j:128 * j + 128],
                        AF.Identity, scale=st(rzc + j)), r=[kE, stk(rzc)], w=[kPV])

            def blk_B(blk):
                E = blk_env(blk)
                pO, kO, Pbuf, Vbuf, kPV, tcol, kT1, kT2 = (E["pO"], E["kO"], E["Pbuf"],
                    E["Vbuf"], E["kPV"], E["tcol"], E["kT1"], E["kT2"])
                for j in range(4):
                    g = 4 * blk + j
                    S.op("tensor", lambda Pbuf=Pbuf, j=j, tcol=tcol, kPV=kPV, kT1=kT1: TE.transpose(
                        psT1[:, tcol + 128 * j:tcol + 128 * j + 128],
                        Pbuf[:, 128 * j:128 * j + 128], ident[:]), r=[kPV, "ident"], w=[kT1])
                    S.op("tensor", lambda g=g, j=j, tcol=tcol, kT2=kT2: TE.transpose(
                        psT2[:, tcol + 128 * j:tcol + 128 * j + 128],
                        v_strip[:, 128 * g:128 * g + 128], ident[:]),
                        r=["v_strip", "v_strip_z", "ident"], w=[kT2])
                S.op("vector", lambda Pbuf=Pbuf, tcol=tcol, kT1=kT1, kPV=kPV: V.tensor_copy(
                     Pbuf[:], psT1[:, tcol:tcol + 512]), r=[kT1], w=[kPV])
                S.op("vector", lambda Vbuf=Vbuf, tcol=tcol, kT2=kT2, kPV=kPV: V.tensor_copy(
                     Vbuf[:], psT2[:, tcol:tcol + 512]), r=[kT2], w=[kPV])
                for j in range(4):
                    S.op("tensor", lambda pO=pO, Pbuf=Pbuf, Vbuf=Vbuf, j=j, kPV=kPV, kO=kO: TE.matmul(
                        pO[:, 128 * j:128 * j + 128], Vbuf[:, 128 * j:128 * j + 128],
                        Pbuf[:, 128 * j:128 * j + 128], start=True, stop=True),
                        r=[kPV], w=[kO])
                S.op("scalar", lambda pO=pO, blk=blk, kO=kO: SC.copy(
                    outSraw[:, 512 * blk:512 * blk + 512], pO[:]), r=[kO], w=["outSraw"])

            blk_A(0); blk_A(1); blk_B(0); blk_A(2); blk_B(1); blk_A(3); blk_B(2); blk_B(3)
            if dbg_on and s == 0:
                dbg_store("outs", outSraw[:], ["outSraw"])
            osv = outSraw[:].rearrange("p (t h) -> p h t", t=128, h=16)
            for pc in range(8):
                pp, kpp = (psS, "psS") if pc < 4 else (psO, "psO")
                col = 128 * (pc % 4)
                S.op("tensor", lambda pp=pp, pc=pc, col=col, osv=osv: TE.matmul(
                    pp[:, col:col + 128], shiftdn[:],
                    osv[:, 2 * pc, :], start=True, stop=False),
                    r=["outSraw", "shiftdn"], w=[kpp])
                S.op("tensor", lambda pp=pp, pc=pc, col=col, osv=osv: TE.matmul(
                    pp[:, col:col + 128], identbot[:],
                    osv[:, 2 * pc + 1, :], start=False, stop=True),
                    r=["outSraw", "identbot"], w=[kpp])
            S.op("vector", lambda: V.tensor_copy(ofmp[:, 0:512], psS[:]), r=["psS"], w=["ofmp"])
            S.op("vector", lambda: V.tensor_copy(ofmp[:, 512:1024], psO[:]), r=["psO"], w=["ofmp"])
            if dbg_on and s == 0:
                dbg_store("ofmp", ofmp[:], ["ofmp"])
            for fh, ps, kps in ((0, psQ1, "psQ1"), (1, psQ2, "psQ2")):
                S.op("tensor", lambda ps=ps, fh=fh: TE.matmul(
                    ps[:], ones_col[:], bo_hi[:, 512 * fh:512 * fh + 512],
                    start=True, stop=False), r=["bo_hi", "ones_col"], w=[kps])
                S.op("tensor", lambda ps=ps, fh=fh: TE.matmul(
                    ps[:], ones_col[:], bo_lo[:, 512 * fh:512 * fh + 512],
                    start=False, stop=False), r=["bo_lo", "ones_col"], w=[kps])
                for kc in range(8):
                    S.op("tensor", lambda ps=ps, fh=fh, kc=kc: TE.matmul(
                        ps[:], ofmp[:, 128 * kc:128 * kc + 128],
                        Wo_sb[:, D * kc + 512 * fh:D * kc + 512 * fh + 512],
                        start=False, stop=(kc == 7)), r=["ofmp", "Wo_sb"], w=[kps])
                S.op("vector", lambda ps=ps, us=us, fh=fh: V.scalar_tensor_tensor(
                    us[:, 512 * fh:512 * fh + 512], ps[:], 1.0,
                    us[:, 512 * fh:512 * fh + 512], ALU.mult, ALU.add),
                    r=[kps, ku], w=[ku])
            if dbg_on and s == 0:
                dbg_store("z", u_ring[:, 0:D], [ku])
            S.op("vector", lambda xs=xs, us=us: V.scalar_tensor_tensor(
                xs, us, 0.5, xs, ALU.mult, ALU.add), r=[ku, kx], w=[kx])
            if feed is not None:
                feed()
            emit_rms(ku, us, r3c)
            S.op("scalar", lambda r3hc=r3hc, r3c=r3c: SC.activation(st(r3hc), st(r3c),
                 AF.Identity, scale=0.5), r=[stk(r3c)], w=[stk(r3hc)])
            S.op("vector", lambda us=us: V.tensor_tensor(dumpB[:], us, g2_m[:], ALU.mult),
                 r=[ku, "g2_m"], w=["dumpB"])
            zslab = zfm[:, D * (s % 8):D * (s % 8) + D]
            kzf = f"zfm{s % 8}"
            for c in range(8):
                S.op("tensor", lambda c=c: TE.transpose(
                    psT2[:, 128 * c:128 * c + 128], dumpB[:, 128 * c:128 * c + 128], ident[:]),
                    r=["dumpB", "ident"], w=["psT2.0", "psT2.1"])
            S.op("vector", lambda zslab=zslab: V.tensor_copy(zslab, psT2[:]),
                 r=["psT2.0", "psT2.1"], w=[kzf])
            S.op("tensor", lambda: TE.matmul(psO[:, 0:8], ones_col[:], bg_hi[:],
                 start=True, stop=False), r=["bg_hi", "ones_col"], w=["psO"])
            S.op("tensor", lambda: TE.matmul(psO[:, 0:8], ones_col[:], bg_lo[:],
                 start=False, stop=False), r=["bg_lo", "ones_col"], w=["psO"])
            for kc in range(8):
                S.op("tensor", lambda zslab=zslab, kc=kc, kzf=kzf: TE.matmul(
                    psO[:, 0:8], zslab[:, 128 * kc:128 * kc + 128],
                    Wg_sb[:, 8 * kc:8 * kc + 8], start=False, stop=(kc == 7)),
                    r=[kzf, "Wg_sb"], w=["psO"])
            S.op("scalar", lambda r3c=r3c: SC.activation(glog[:], psO[:, 0:8], AF.Identity,
                 scale=st(r3c)), r=["psO", stk(r3c)], w=["glog"])
            S.op("vector", lambda gtc=gtc: V.tensor_reduce(st(gtc), glog[:], AX.X, ALU.max,
                 negate=True), r=["glog"], w=[stk(gtc)])
            S.op("scalar", lambda gtc=gtc, zgc=zgc: SC.activation(glog[:], glog[:], AF.Exp,
                 bias=st(gtc), accum_out=st(zgc)), r=["glog", stk(gtc)], w=["glog", stk(zgc)])
            S.op("vector", lambda zgc=zgc: V.reciprocal(st(zgc), st(zgc)),
                 r=[stk(zgc)], w=[stk(zgc)])
            sgw = gw[:, 8 * (s % 8):8 * (s % 8) + 8]
            S.op("vector", lambda sgw=sgw, zgc=zgc, r3hc=r3hc: V.tensor_scalar(sgw, glog[:],
                 st(zgc), st(r3hc), ALU.mult, ALU.mult),
                 r=["glog", stk(zgc), stk(r3hc)], w=[f"gw{s % 8}"])

        def emit_expert_unit(e, fh, slabs):
            slot = (2 * e + fh) % 2
            wslot = We_sb[:, 4096 * slot:4096 * slot + 4096]
            kws = f"we{slot}"
            wsrc = dWe[e].rearrange("(c p) f -> p c f", p=128)[:, :, 512 * fh:512 * fh + 512]
            S.dma("gpsimd", lambda wslot=wslot, wsrc=wsrc: G.dma_start(
                wslot.rearrange("p (c f) -> p c f", c=8), wsrc), w=[kws])
            for s in slabs:
                ps, kps = (psK1, "psK1") if (s % 2 == 0) else (psK2, "psK2")
                zslab = zfm[:, D * (s % 8):D * (s % 8) + D]
                for kc in range(8):
                    S.op("tensor", lambda ps=ps, zslab=zslab, wslot=wslot, kc=kc, s=s: TE.matmul(
                        ps[:], zslab[:, 128 * kc:128 * kc + 128],
                        wslot[:, 512 * kc:512 * kc + 512],
                        start=(kc == 0), stop=(kc == 7)),
                        r=[f"zfm{s % 8}", kws], w=[kps])
                xsl = x_res[:, D * s + 512 * fh:D * s + 512 * fh + 512]
                S.op("vector", lambda ps=ps, xsl=xsl, s=s, e=e: V.scalar_tensor_tensor(
                    xsl, ps[:], gw[:, 8 * (s % 8) + e:8 * (s % 8) + e + 1], xsl,
                    ALU.mult, ALU.add), r=[kps, f"gw{s % 8}"], w=[f"x{s}"])

        def emit_ffn(fid, dbg_on):
            if fid == 1:
                S.op("vector", lambda: V.memset(q_strip[:], 0.0), w=["q_strip"])
                S.op("vector", lambda: V.memset(v_strip[0:64, :], 0.0), w=["v_strip_z", "v_strip"])
            mkfeed = lambda s: (lambda: emit_slab_head(s + 1, fid, dbg_on)) if s + 1 < NS else None
            emit_slab_head(0, fid, dbg_on)
            for s in range(8):
                emit_slab_body(s, fid, dbg_on, feed=mkfeed(s))
            sl0 = list(range(8))
            for e in range(8):
                for fh in range(2):
                    emit_expert_unit(e, fh, sl0)
            for s in range(8, 16):
                emit_slab_body(s, fid, dbg_on, feed=mkfeed(s))
            if dbg_on:
                dbg_store("gate", gw[:], [f"gw{i}" for i in range(8)])
                dbg_store("zfm", zfm[:], [f"zfm{i}" for i in range(8)])
            sl1 = list(range(8, 16))
            return [(lambda e=e, fh=fh: emit_expert_unit(e, fh, sl1))
                    for e in range(8) for fh in range(2)]

        def emit_attn_part1(dbg_on, feeder=()):
            S.dma("gpsimd", lambda: G.dma_start(We_sb[:].rearrange("p (c f) -> p c f", c=8), dWr.rearrange("(c p) f -> p c f", p=128)),
                  w=["we0", "we1"])
            for c in range(8):
                for oc in range(8):
                    S.op("tensor", lambda c=c, oc=oc: TE.transpose(
                        psT1[:, 0:128], We_sb[:, D * oc + 128 * c:D * oc + 128 * c + 128], ident[:]),
                        r=["we0", "we1", "ident"], w=["psT1.0", "psT1.1"])
                    S.op("vector", lambda c=c, oc=oc: V.tensor_copy(
                        Wq_sb[:, D * c + 128 * oc:D * c + 128 * oc + 128], psT1[:, 0:128]),
                        r=["psT1.0", "psT1.1"], w=["Wq_sb"])
            S.dma("gpsimd", lambda: G.dma_start(We_sb[:].rearrange("p (c f) -> p c f", c=8), dWf.rearrange("(c p) f -> p c f", p=128)),
                  w=["we0", "we1"])
            for c in range(8):
                for oc in range(8):
                    S.op("tensor", lambda c=c, oc=oc: TE.transpose(
                        psT1[:, 0:128], We_sb[:, D * oc + 128 * c:D * oc + 128 * c + 128], ident[:]),
                        r=["we0", "we1", "ident"], w=["psT1.0", "psT1.1"])
                    S.op("vector", lambda c=c, oc=oc: V.tensor_copy(
                        Wo_sb[:, D * c + 128 * oc:D * c + 128 * oc + 128], psT1[:, 0:128]),
                        r=["psT1.0", "psT1.1"], w=["Wo_sb"])
            bcast_rows(lng_m, "q_strip", dlng)
            bcast_rows(lnb_m, "k_strip", dlnb)
            S.dma("sync", lambda: SY.dma_start(row_f[:, 0:128], dattr), w=["row_f"])
            S.op("vector", lambda: V.tensor_copy(row_hi[:, 0:128], row_f[:, 0:128]),
                 r=["row_f"], w=["row_hi"])
            for rep in range(8):
                S.op("tensor", lambda rep=rep: TE.matmul(psS[:, 0:128], ones_col[:],
                     row_hi[:, 0:128], start=True, stop=True), r=["row_hi", "ones_col"], w=["psS"])
                S.op("vector", lambda rep=rep: V.tensor_copy(
                    attr_m[:, 128 * rep:128 * rep + 128], psS[:, 0:128]), r=["psS"], w=["outSraw"])
            def a1_head(s):
                xs = x_res[:, D * s:D * s + D]
                us = u_ring[:, D * (s % 2):D * (s % 2) + D]
                kx, ku = f"x{s}", f"u{s % 2}"
                emit_ln_u(s, kx, xs, ku, us)
                if dbg_on and s == 0:
                    dbg_store("u2", u_ring[:, 0:D], [ku])
                hs = h_tok[:, D * (s % 2):D * (s % 2) + D]
                S.op("scalar", lambda us=us, hs=hs, s=s: SC.copy(hs, us),
                     r=[ku], w=[f"h{s % 2}"])

            a1_head(0)
            for s in range(NS):
                hs = h_tok[:, D * (s % 2):D * (s % 2) + D]
                kh = f"h{s % 2}"
                hfm = h_fm[:, D * (s % 2):D * (s % 2) + D]
                khf = f"hfm{s % 2}"
                for c in range(8):
                    S.op("tensor", lambda hs=hs, c=c: TE.transpose(
                        psT2[:, 128 * c:128 * c + 128], hs[:, 128 * c:128 * c + 128], ident[:]),
                        r=[kh, "ident"], w=["psT2.0", "psT2.1"])
                S.op("vector", lambda hfm=hfm: V.tensor_copy(hfm, psT2[:]), r=["psT2.0", "psT2.1"], w=[khf])
                for fh, ps, kps in ((0, psQ1, "psQ1"), (1, psQ2, "psQ2")):
                    for kc in range(8):
                        S.op("tensor", lambda ps=ps, hfm=hfm, fh=fh, kc=kc: TE.matmul(
                            ps[:], hfm[:, 128 * kc:128 * kc + 128],
                            Wq_sb[:, D * kc + 512 * fh:D * kc + 512 * fh + 512],
                            start=(kc == 0), stop=(kc == 7)), r=[khf, "Wq_sb"], w=[kps])
                if s < 8:
                    for u in feeder[2 * s:2 * s + 2]:
                        u()
                if s + 1 < NS:
                    a1_head(s + 1)
                S.op("scalar", lambda: SC.copy(ofmp[:, 0:512], psQ1[:]), r=["psQ1"], w=["ofmp"])
                S.op("scalar", lambda: SC.copy(ofmp[:, 512:1024], psQ2[:]), r=["psQ2"], w=["ofmp"])
                S.dma("sync", lambda s=s: SY.dma_start(FRS[:, 1024 * s:1024 * s + 1024], ofmp[:]),
                      r=["ofmp"])
                S.op("scalar", lambda: SC.activation(dumpA[:, 0:512], psQ1[:], AF.Lrelu, alpha=0.01),
                     r=["psQ1"], w=["dumpA"])
                S.op("scalar", lambda: SC.activation(dumpA[:, 512:1024], psQ2[:], AF.Lrelu, alpha=0.01),
                     r=["psQ2"], w=["dumpA"])
                S.op("vector", lambda: V.tensor_tensor(dumpB[:], dumpA[:], attr_m[:], ALU.mult),
                     r=["dumpA", "outSraw"], w=["dumpB"])
                S.op("vector", lambda: V.tensor_reduce(srtok_f[:], dumpB[:].rearrange(
                     "p (h d) -> p h d", h=8), AX.X, ALU.add), r=["dumpB"], w=["srtok_f"])
                S.op("scalar", lambda: SC.copy(srtok_bf[:], srtok_f[:]),
                     r=["srtok_f"], w=["srtok_bf"])
                S.op("tensor", lambda: TE.transpose(psT1[0:8, 0:128], srtok_bf[:], ident[:]),
                     r=["srtok_bf", "ident"], w=["psT1.0", "psT1.1"])
                S.op("vector", lambda s=s: V.tensor_copy(sr_all[:, 128 * s:128 * s + 128],
                     psT1[0:8, 0:128]), r=["psT1.0", "psT1.1"], w=[f"sr{s}", "pv0", "pv1"])
            srk = [f"sr{s}" for s in range(NS)]
            if dbg_on:
                dbg_store("sr", sr_all[:], srk)
            S.op("vector", lambda: V.tensor_reduce(stats[0:8, c_pm:c_pm + 1], sr_all[:], AX.X,
                 ALU.max, negate=True), r=srk + ["pv0", "pv1"], w=[stk(c_pm)])
            S.op("scalar", lambda: SC.activation(sr_all[:], sr_all[:], AF.Exp,
                 bias=stats[0:8, c_pm:c_pm + 1], accum_out=stats[0:8, c_pz:c_pz + 1]),
                 r=[stk(c_pm)] + srk, w=[stk(c_pz)] + srk + ["pv0", "pv1"])
            S.op("vector", lambda: V.reciprocal(stats[0:8, c_pz:c_pz + 1],
                 stats[0:8, c_pz:c_pz + 1]), r=[stk(c_pz)], w=[stk(c_pz)])
            S.op("scalar", lambda: SC.activation(sr_all[:], sr_all[:], AF.Identity,
                 scale=stats[0:8, c_pz:c_pz + 1]), r=[stk(c_pz)] + srk, w=srk + ["pv0", "pv1"])
            if dbg_on:
                dbg_store("p", sr_all[:], srk)
            for s in range(NS):
                S.op("tensor", lambda s=s: TE.transpose(psT2[:, 8 * s:8 * s + 8],
                     sr_all[0:8, 128 * s:128 * s + 128], ident[0:8, 0:8]), r=[f"sr{s}", "pv0", "pv1", "ident"], w=["psT2.0", "psT2.1"])
            S.op("vector", lambda: V.tensor_copy(pT[:], psT2[:, 0:128]), r=["psT2.0", "psT2.1"], w=["pT"])
            for s in range(NS):
                slot = s % 2
                fbuf = h_fm[:, D * slot:D * slot + D]
                kf = f"hfm{slot}"
                S.dma("sync", lambda s=s, fbuf=fbuf: SY.dma_start(fbuf, FRS[:, 1024 * s:1024 * s + 1024]),
                      w=[kf])
                for h in range(8):
                    S.op("tensor", lambda h=h, s=s, fbuf=fbuf, kf=kf: TE.matmul(
                        psK1[:, h:h + 1], fbuf[:, 128 * h:128 * h + 128],
                        pT[:, 8 * s + h:8 * s + h + 1], start=(s == 0 and h == 0),
                        stop=(s == NS - 1)), r=[kf, "pT"], w=["psK1"])
            S.op("vector", lambda: V.tensor_copy(ctx_bf[:], psK1[:, 0:8]), r=["psK1"], w=["ctx_bf"])
            if dbg_on:
                dbg_store("ctx", ctx_bf[:], ["ctx_bf"])
            for oc in range(8):
                for c in range(8):
                    S.op("tensor", lambda oc=oc, c=c: TE.matmul(
                        psK2[:, oc:oc + 1], Wo_sb[:, D * c + 128 * oc:D * c + 128 * oc + 128],
                        ctx_bf[:, c:c + 1], start=(c == 0), stop=(c == 7)),
                        r=["Wo_sb", "ctx_bf"], w=["psK2"])
            S.op("vector", lambda: V.tensor_copy(fhFM_bf[:], psK2[:, 0:8]), r=["psK2"], w=["fhFM_bf"])
            S.op("tensor", lambda: TE.transpose(psT1[0:8, 0:128], fhFM_bf[:], ident[:]),
                 r=["fhFM_bf", "ident"], w=["psT1.0", "psT1.1"])
            S.op("vector", lambda: V.tensor_copy(fhT8[:], psT1[0:8, 0:128]), r=["psT1.0", "psT1.1"], w=["fhT8"])
            for c in range(8):
                S.dma("gpsimd", lambda c=c: G.dma_start(row_hi[:, 128 * c:128 * c + 128],
                      fhT8[c:c + 1, :]), r=["fhT8"], w=["row_hi"])
            for hh in range(0, D, 512):
                S.op("tensor", lambda hh=hh: TE.matmul(psS[:], ones_col[:],
                     row_hi[:, hh:hh + 512], start=True, stop=True),
                     r=["row_hi", "ones_col"], w=["psS"])
                S.op("vector", lambda hh=hh: V.tensor_copy(fh_m[:, hh:hh + 512], psS[:]),
                     r=["psS"], w=["outSraw"])
            if dbg_on:
                dbg_store("fh", fh_m[:], ["outSraw"])

        def emit_attn_part2(dbg_on):
            for s in range(NS):
                xs = x_res[:, D * s:D * s + D]
                uslot = s % 2
                sp = s % 2
                us = u_ring[:, D * uslot:D * uslot + D]
                kx, ku = f"x{s}", f"u{uslot}"
                lsum, lmu, lv, lt, lr = (cL_sum[sp], cL_mu[sp], cL_v[sp], cL_t[sp], cL_r[sp])
                ymuc, yrc = c_ymu[sp], c_yr[sp]
                emit_ln_u(s, kx, xs, ku, us)
                S.op("vector", lambda us=us: V.tensor_tensor(us, us, fh_m[:], ALU.add),
                     r=[ku, "outSraw"], w=[ku])
                S.op("vector", lambda us=us, lsum=lsum: V.tensor_reduce(st(lsum), us, AX.X, ALU.add),
                     r=[ku], w=[stk(lsum)])
                S.op("scalar", lambda us=us, lv=lv: SC.activation(dumpA[:], us, AF.Square,
                     accum_out=st(lv)), r=[ku], w=["dumpA", stk(lv)])
                S.op("scalar", lambda ymuc=ymuc, lsum=lsum: SC.activation(st(ymuc), st(lsum),
                     AF.Identity, scale=1.0 / D), r=[stk(lsum)], w=[stk(ymuc)])
                S.op("scalar", lambda lt=lt, ymuc=ymuc: SC.activation(st(lt), st(ymuc), AF.Square),
                     r=[stk(ymuc)], w=[stk(lt)])
                S.op("vector", lambda lv=lv, lt=lt: V.tensor_scalar(st(lv), st(lv), 1.0 / D, st(lt),
                     ALU.mult, ALU.subtract), r=[stk(lv), stk(lt)], w=[stk(lv)])
                S.op("scalar", lambda lv=lv: SC.activation(st(lv), st(lv), AF.Ln, bias=eps_ln[:]),
                     r=[stk(lv), "eps_ln"], w=[stk(lv)])
                S.op("scalar", lambda yrc=yrc, lv=lv: SC.activation(st(yrc), st(lv), AF.Exp, scale=-0.5),
                     r=[stk(lv)], w=[stk(yrc)])
                S.op("vector", lambda us=us, ymuc=ymuc, yrc=yrc: V.tensor_scalar(us, us, st(ymuc), st(yrc),
                     ALU.subtract, ALU.mult), r=[ku, stk(ymuc), stk(yrc)], w=[ku])
                S.op("gpsimd", lambda us=us: G.tensor_tensor(us, us, lng_m[:], ALU.mult),
                     r=[ku, "q_strip"], w=[ku])
                S.op("gpsimd", lambda us=us: G.tensor_tensor(us, us, lnb_m[:], ALU.add),
                     r=[ku, "k_strip"], w=[ku])
                S.op("vector", lambda xs=xs, us=us: V.tensor_tensor(xs, xs, us, ALU.add),
                     r=[ku, kx], w=[kx])
            if dbg_on:
                dbg_store("x2", x_res[:, 0:D], ["x0"])

        DBG = bool(debug)
        load_ffn_weights()
        dbg_store("wup", Wup_sb[:], ["Wup_sb"])
        dbg_store("wdn", Wdown_sb[:], ["Wdown_sb"])
        dbg_store("wq", Wq_sb[:], ["Wq_sb"])
        dbg_store("bupk", bupk_rows[:], ["bupk"])
        dbg_store("bupv", bupv_rows[:], ["bupv"])
        dbg_store("wvp", Wup_vpad[:], ["Wup_vpad"])
        pend1 = emit_ffn(0, DBG)
        if DBG:
            dbg_store("x1", x_res[:, 0:D], ["x0"])
        emit_attn_part1(DBG, feeder=pend1)
        load_ffn_weights()
        emit_attn_part2(DBG)
        pend2 = emit_ffn(1, False)
        for u in pend2:
            u()

        for s in range(NS):
            src = x_res[:, D * s:D * s + D]
            S.dma("sync", lambda s=s, src=src: SY.dma_start(OUT[128 * s:128 * s + 128, :], src),
                  r=[f"x{s}"])

        S.finalize(nc, ctx)
        block = ctx.enter_context(nc.Block())

        @block.tensor
        def _(te):
            S.emit_engine("tensor", te)

        @block.vector
        def _(ve):
            S.emit_engine("vector", ve)

        @block.scalar
        def _(se):
            S.emit_engine("scalar", se)

        @block.gpsimd
        def _(ge):
            S.emit_engine("gpsimd", ge)
            S.final_wait("gpsimd", ge)

        @block.sync
        def _(sy):
            S.emit_engine("sync", sy)
            S.final_wait("sync", sy)

    return nc


_NC_CACHE = {}


def _get_nc(debug=()):
    key = tuple(debug)
    if key not in _NC_CACHE:
        _NC_CACHE[key] = build_nc(debug)
    return _NC_CACHE[key]


def make_in_maps(inputs):
    x = np.ascontiguousarray(np.asarray(inputs["L0_attn_h"], dtype=np.float32))
    common = dict(
        Wq=np.ascontiguousarray(inputs["L0_ffn_Wq"], np.float32),
        bq=np.ascontiguousarray(np.asarray(inputs["L0_ffn_bq"], np.float32).reshape(1, -1)),
        Wdown=np.ascontiguousarray(inputs["L0_ffn_Wdown"], np.float32),
        bdown=np.ascontiguousarray(np.asarray(inputs["L0_ffn_bdown"], np.float32).reshape(1, -1)),
        Wup=np.ascontiguousarray(inputs["L0_ffn_Wup"], np.float32),
        bup=np.ascontiguousarray(np.asarray(inputs["L0_ffn_bup"], np.float32).reshape(1, -1)),
        Wo=np.ascontiguousarray(inputs["L0_ffn_Wo"], np.float32),
        bo=np.ascontiguousarray(np.asarray(inputs["L0_ffn_bo"], np.float32).reshape(1, -1)),
        Wg=np.ascontiguousarray(inputs["L0_ffn_Wg"], np.float32),
        bg=np.ascontiguousarray(np.asarray(inputs["L0_ffn_bg"], np.float32).reshape(1, -1)),
        We=np.ascontiguousarray(inputs["L0_ffn_We"], np.float32),
        gamma1=np.ascontiguousarray(np.asarray(inputs["L0_ffn_gamma1"], np.float32).reshape(1, -1)),
        gamma2=np.ascontiguousarray(np.asarray(inputs["L0_ffn_gamma2"], np.float32).reshape(1, -1)),
        Wr=np.ascontiguousarray(inputs["L0_attn_Wr"], np.float32),
        Wf=np.ascontiguousarray(inputs["L0_attn_Wf"], np.float32),
        att_r=np.ascontiguousarray(np.asarray(inputs["L0_attn_att_r"], np.float32).reshape(1, -1)),
        ln_gamma=np.ascontiguousarray(np.asarray(inputs["L0_attn_ln_gamma"], np.float32).reshape(1, -1)),
        ln_beta=np.ascontiguousarray(np.asarray(inputs["L0_attn_ln_beta"], np.float32).reshape(1, -1)),
    )
    return [dict(common, x=np.ascontiguousarray(x[b])) for b in range(x.shape[0])]


def kernel(**inputs):
    in_maps = make_in_maps(inputs)
    nc = _get_nc()
    res = run_bass_kernel_spmd(nc, in_maps, core_ids=list(range(8)))
    return np.stack([r["out"] for r in res.results], axis=0)


# version 15
# speedup vs baseline: 1.0082x; 1.0082x over previous
import numpy as np
import ml_dtypes
import concourse.bass as bass
import concourse.mybir as mybir
from concourse.bass_utils import run_bass_kernel_spmd

F32 = mybir.dt.float32
BF16 = mybir.dt.bfloat16
AF = mybir.ActivationFunctionType
ALU = mybir.AluOpType
AX = mybir.AxisListType

T = 2048
NS = 16
D = 1024
NEG = -1e30
MC = 1e30
LN_EPS = 1e-5
RMS_EPS = 1e-6

DRAIN_BEFORE_INC = False

ENGINES = ("tensor", "vector", "scalar", "gpsimd", "sync")


class Sched:
    def __init__(self):
        self.ops = {e: [] for e in ENGINES}
        self.nq = {"sync": 0, "gpsimd": 0}
        self.last_w = {}
        self.readers = {}

    def op(self, eng, fn, r=(), w=()):
        idx = len(self.ops[eng])
        ident = ("e", eng, idx)
        deps, raw = self._deps(r, w, ident)
        self.ops[eng].append(dict(fn=fn, deps=deps, raw=raw, inc=False, dma=None))
        return ident

    def dma(self, eng, fn, r=(), w=()):
        qidx = self.nq[eng]
        self.nq[eng] += 1
        ident = ("q", eng, qidx)
        deps, raw = self._deps(r, w, ident)
        self.ops[eng].append(dict(fn=fn, deps=deps, raw=raw, inc=False, dma=qidx))
        return ident

    def _deps(self, r, w, ident):
        deps = []
        raw = []
        for k in r:
            lw = self.last_w.get(k)
            if lw is not None:
                deps.append(lw)
                raw.append(lw)
            self.readers.setdefault(k, []).append(ident)
        for k in w:
            lw = self.last_w.get(k)
            if lw is not None:
                deps.append(lw)
            deps.extend(self.readers.get(k, ()))
            self.last_w[k] = ident
            self.readers[k] = []
        return deps, raw

    def finalize(self, nc, ctx):
        self.esem = {e: ctx.enter_context(nc.semaphore(f"sem_{e}")) for e in ENGINES}
        self.qsem = {e: ctx.enter_context(nc.semaphore(f"qsem_{e}")) for e in self.nq}
        for eng in ENGINES:
            for o in self.ops[eng]:
                for kind, seng, sidx in o["deps"]:
                    if kind == "e" and seng != eng:
                        self.ops[seng][sidx]["inc"] = True
        pref = {}
        for eng in ENGINES:
            c = 0
            pl = []
            for o in self.ops[eng]:
                if o["inc"]:
                    c += 1
                pl.append(c)
            pref[eng] = pl
        for eng in ENGINES:
            seen = {}
            for o in self.ops[eng]:
                waits = {}
                for kind, seng, sidx in o["deps"]:
                    if kind == "e":
                        if seng == eng:
                            continue
                        sem, val = ("e", seng), pref[seng][sidx]
                        if val == 0:
                            continue
                    else:
                        if seng == eng and o["dma"] is not None:
                            continue
                        sem, val = ("q", seng), 16 * (sidx + 1)
                    if waits.get(sem, 0) < val:
                        waits[sem] = val
                ww = []
                for sem, val in sorted(waits.items(), key=str):
                    if seen.get(sem, 0) >= val:
                        continue
                    seen[sem] = val
                    ww.append((sem, val))
                o["waits"] = ww

    def emit_engine(self, eng, engobj):
        drained_upto = -1
        for i, o in enumerate(self.ops[eng]):
            if eng == "vector":
                same = [sidx for kind, seng, sidx in o["raw"]
                        if kind == "e" and seng == eng]
                if same and max(same) > drained_upto:
                    engobj.drain()
                    drained_upto = i - 1
            for (sk, sname), val in o["waits"]:
                sem = self.esem[sname] if sk == "e" else self.qsem[sname]
                engobj.wait_ge(sem, val)
            ins = o["fn"]()
            if o["dma"] is not None:
                ins.then_inc(self.qsem[eng], 16)
            elif o["inc"]:
                if eng == "vector" and DRAIN_BEFORE_INC:
                    ins = engobj.drain()
                    drained_upto = i
                ins.then_inc(self.esem[eng], 1)

    def final_wait(self, eng, engobj):
        if self.nq.get(eng, 0):
            engobj.wait_ge(self.qsem[eng], 16 * self.nq[eng])


def build_nc(debug=()):
    nc = bass.Bass("TRN2", target_bir_lowering=False)
    S = Sched()

    X = nc.dram_tensor("x", [T, D], F32, kind="ExternalInput").ap()
    dWq = nc.dram_tensor("Wq", [D, D], F32, kind="ExternalInput").ap()
    dbq = nc.dram_tensor("bq", [1, D], F32, kind="ExternalInput").ap()
    dWdown = nc.dram_tensor("Wdown", [D, 64], F32, kind="ExternalInput").ap()
    dbdown = nc.dram_tensor("bdown", [1, 64], F32, kind="ExternalInput").ap()
    dWup = nc.dram_tensor("Wup", [64, 2 * D], F32, kind="ExternalInput").ap()
    dbup = nc.dram_tensor("bup", [1, 2 * D], F32, kind="ExternalInput").ap()
    dWo = nc.dram_tensor("Wo", [D, D], F32, kind="ExternalInput").ap()
    dbo = nc.dram_tensor("bo", [1, D], F32, kind="ExternalInput").ap()
    dWg = nc.dram_tensor("Wg", [D, 8], F32, kind="ExternalInput").ap()
    dbg = nc.dram_tensor("bg", [1, 8], F32, kind="ExternalInput").ap()
    dWe = nc.dram_tensor("We", [8, D, D], F32, kind="ExternalInput").ap()
    dg1 = nc.dram_tensor("gamma1", [1, D], F32, kind="ExternalInput").ap()
    dg2 = nc.dram_tensor("gamma2", [1, D], F32, kind="ExternalInput").ap()
    dWr = nc.dram_tensor("Wr", [D, D], F32, kind="ExternalInput").ap()
    dWf = nc.dram_tensor("Wf", [D, D], F32, kind="ExternalInput").ap()
    dattr = nc.dram_tensor("att_r", [1, 128], F32, kind="ExternalInput").ap()
    dlng = nc.dram_tensor("ln_gamma", [1, D], F32, kind="ExternalInput").ap()
    dlnb = nc.dram_tensor("ln_beta", [1, D], F32, kind="ExternalInput").ap()
    FRS = nc.dram_tensor("fr_scratch", [128, NS * 1024], BF16).ap()
    OUT = nc.dram_tensor("out", [T, D], F32, kind="ExternalOutput").ap()
    dbgout = {}
    for name in debug:
        shape = {"u": [128, D], "h": [128, D], "q": [128, 2048], "k": [128, 2048],
                 "v": [128, 2048], "outs": [128, 2048], "z": [128, D],
                 "x1": [128, D], "x2": [128, D], "gate": [128, 64],
                 "u2": [128, D], "sr": [8, T], "p": [8, T],
                 "ctx": [128, 8], "fh": [128, D], "ofmp": [128, D],
                 "wup": [128, 2048], "wdn": [128, 512], "wq": [128, 8 * D],
                 "bupk": [1, 2048], "bupv": [1, 2048], "wvp": [128, 2048],
                 "d0": [128, 128], "wdnf": [128, 512], "zfm": [128, 8192]}[name]
        dbgout[name] = nc.dram_tensor("dbg_" + name, shape, F32, kind="ExternalOutput").ap()

    from contextlib import ExitStack
    ctx = ExitStack()
    with ctx:
        sbt = lambda n, s, d=F32: ctx.enter_context(nc.sbuf_tensor(n, s, d))
        x_res = sbt("x_res", [128, NS * D])
        Wq_sb = sbt("Wq_sb", [128, 8 * D], BF16)
        Wo_sb = sbt("Wo_sb", [128, 8 * D], BF16)
        We_sb = sbt("We_sb", [128, 8192], BF16)
        Wdown_sb = sbt("Wdown_sb", [128, 8 * 64], BF16)
        Wup_sb = sbt("Wup_sb", [128, 2048], BF16)
        Wup_vpad = sbt("Wup_vpad", [128, 2048], BF16)
        Wg_sb = sbt("Wg_sb", [128, 64], BF16)
        g1_m = sbt("g1_m", [128, D], BF16)
        g2_m = sbt("g2_m", [128, D], BF16)
        bq_rows = sbt("bq_rows", [1, 1024], BF16)
        bdown_row = sbt("bdown_row", [1, 64], BF16)
        bupk_rows = sbt("bupk_rows", [1, 2048], BF16)
        bupv_rows = sbt("bupv_rows", [1, 2048], BF16)
        bo_hi = sbt("bo_hi", [1, D], BF16)
        bo_lo = sbt("bo_lo", [1, D], BF16)
        bg_hi = sbt("bg_hi", [1, 8], BF16)
        bg_lo = sbt("bg_lo", [1, 8], BF16)
        row_f = sbt("row_f", [1, 512])
        row_hi = sbt("row_hi", [1, D], BF16)
        ones_col = sbt("ones_col", [1, 128], BF16)
        ident = sbt("ident", [128, 128], BF16)
        shiftdn = sbt("shiftdn", [128, 128], BF16)
        identbot = sbt("identbot", [128, 128], BF16)
        m9l = sbt("m9l", [9, 128], BF16)
        m9r = sbt("m9r", [9, 512], BF16)
        u_ring = sbt("u_ring", [128, 2 * D])
        h_tok = sbt("h_tok", [128, 2 * D], BF16)
        h_fm = sbt("h_fm", [128, 2 * D], BF16)
        zfm = sbt("zfm", [128, 8 * D], BF16)
        q_strip = sbt("q_strip", [128, 2048], BF16)
        k_strip = sbt("k_strip", [128, 2048], BF16)
        v_strip = sbt("v_strip", [128, 2048], BF16)
        outSraw = sbt("outSraw", [128, 2048], BF16)
        ofmp = sbt("ofmp", [128, D], BF16)
        dumpA = sbt("dumpA", [128, D], BF16)
        dumpB = sbt("dumpB", [128, D], BF16)
        dumpC = sbt("dumpC", [128, D], BF16)
        pv_buf = sbt("pv_buf", [128, 2 * D], BF16)
        sr_all = pv_buf[0:8, 0:T]
        stats = sbt("stats", [128, 64])
        gw = sbt("gw", [128, 64])
        glog = sbt("glog", [128, 8])
        pT = sbt("pT", [128, 128], BF16)
        ctx_bf = sbt("ctx_bf", [128, 8], BF16)
        fhFM_bf = sbt("fhFM_bf", [128, 8], BF16)
        fhT8 = sbt("fhT8", [8, 128], BF16)
        srtok_bf = sbt("srtok_bf", [128, 8], BF16)
        srtok_f = sbt("srtok_f", [128, 8])
        eps_ln = sbt("eps_ln", [128, 1])
        eps_rms = sbt("eps_rms", [128, 1])

        psQ1 = ctx.enter_context(nc.psum_tensor("psQ1", [128, 512], F32))
        psQ2 = ctx.enter_context(nc.psum_tensor("psQ2", [128, 512], F32))
        psK1 = ctx.enter_context(nc.psum_tensor("psK1", [128, 512], F32))
        psK2 = ctx.enter_context(nc.psum_tensor("psK2", [128, 512], F32))
        psS = ctx.enter_context(nc.psum_tensor("psS", [128, 512], F32))
        psO = ctx.enter_context(nc.psum_tensor("psO", [128, 512], F32))
        psT1 = ctx.enter_context(nc.psum_tensor("psT1", [128, 1024], BF16))
        psT2 = ctx.enter_context(nc.psum_tensor("psT2", [128, 1024], BF16))

        lng_m = q_strip[:].bitcast(F32)
        lnb_m = k_strip[:].bitcast(F32)
        fh_m = outSraw[:, 0:1024]
        attr_m = outSraw[:, 1024:2048]

        V, SC, TE, G, SY = nc.vector, nc.scalar, nc.tensor, nc.gpsimd, nc.sync

        stat_alloc = [0]

        def stcols(n):
            c = stat_alloc[0]
            stat_alloc[0] += n
            assert stat_alloc[0] <= 64, "stats overflow"
            return c

        def st(c):
            return stats[:, c:c + 1]

        def stk(c):
            return f"stc{c}"

        def dbg_store(name, src_ap, rkeys):
            if name in dbgout:
                S.dma("gpsimd", lambda: G.dma_start(dbgout[name], src_ap), r=list(rkeys))

        def bcast_rows(dst_sb, dst_key, src_dram, n=D):
            for hh in range(0, n, 512):
                nn = min(512, n - hh)
                S.dma("sync", lambda hh=hh, nn=nn: SY.dma_start(row_f[:, 0:nn],
                      src_dram[:, hh:hh + nn]), w=["row_f"])
                S.op("vector", lambda hh=hh, nn=nn: V.tensor_copy(row_hi[:, hh:hh + nn],
                     row_f[:, 0:nn]), r=["row_f"], w=["row_hi"])
                S.op("tensor", lambda hh=hh, nn=nn: TE.matmul(psS[:, 0:nn], ones_col[:],
                     row_hi[:, hh:hh + nn], start=True, stop=False),
                     r=["row_hi", "ones_col"], w=["psS"])
                S.op("vector", lambda hh=hh, nn=nn: V.tensor_tensor(row_f[:, 0:nn], row_f[:, 0:nn],
                     row_hi[:, hh:hh + nn], ALU.subtract), r=["row_hi"], w=["row_f"])
                S.op("vector", lambda hh=hh, nn=nn: V.tensor_copy(row_hi[:, hh:hh + nn],
                     row_f[:, 0:nn]), r=["row_f"], w=["row_hi"])
                S.op("tensor", lambda hh=hh, nn=nn: TE.matmul(psS[:, 0:nn], ones_col[:],
                     row_hi[:, hh:hh + nn], start=False, stop=True),
                     r=["row_hi", "ones_col"], w=["psS"])
                S.op("vector", lambda hh=hh, nn=nn: V.tensor_copy(dst_sb[:, hh:hh + nn], psS[:, 0:nn]),
                     r=["psS"], w=[dst_key])

        def bcast_bf16(dst_sb, dst_key, src_dram, n=D):
            for hh in range(0, n, 512):
                nn = min(512, n - hh)
                S.dma("sync", lambda hh=hh, nn=nn: SY.dma_start(row_f[:, 0:nn],
                      src_dram[:, hh:hh + nn]), w=["row_f"])
                S.op("vector", lambda hh=hh, nn=nn: V.tensor_copy(row_hi[:, hh:hh + nn],
                     row_f[:, 0:nn]), r=["row_f"], w=["row_hi"])
            for hh in range(0, n, 512):
                nn = min(512, n - hh)
                S.op("tensor", lambda hh=hh, nn=nn: TE.matmul(psS[:, 0:nn], ones_col[:],
                     row_hi[:, hh:hh + nn], start=True, stop=True),
                     r=["row_hi", "ones_col"], w=["psS"])
                S.op("vector", lambda hh=hh, nn=nn: V.tensor_copy(dst_sb[:, hh:hh + nn], psS[:, 0:nn]),
                     r=["psS"], w=[dst_key])

        def hilo_row(dst_hi, dst_lo, khi, klo, src_dram, n):
            for hh in range(0, n, 512):
                nn = min(512, n - hh)
                S.dma("sync", lambda hh=hh, nn=nn: SY.dma_start(row_f[:, 0:nn],
                      src_dram[:, hh:hh + nn]), w=["row_f"])
                S.op("vector", lambda hh=hh, nn=nn: V.tensor_copy(dst_hi[:, hh:hh + nn],
                     row_f[:, 0:nn]), r=["row_f"], w=[khi])
                S.op("vector", lambda hh=hh, nn=nn: V.tensor_tensor(row_f[:, 0:nn], row_f[:, 0:nn],
                     dst_hi[:, hh:hh + nn], ALU.subtract), r=[khi], w=["row_f"])
                S.op("vector", lambda hh=hh, nn=nn: V.tensor_copy(dst_lo[:, hh:hh + nn],
                     row_f[:, 0:nn]), r=["row_f"], w=[klo])

        Xv = X.rearrange("(s p) f -> p s f", p=128)
        S.dma("sync", lambda: SY.dma_start(x_res[:].rearrange("p (s f) -> p s f", s=NS), Xv),
              w=[f"x{s}" for s in range(NS)])

        def g_masks():
            G.memset(ident[:], 0.0)
            G.affine_select(out=ident[:], in_=ident[:], compare_op=ALU.not_equal,
                            fill=1.0, base=0, pattern=[[-1, 128]], channel_multiplier=1)
            G.memset(shiftdn[:], 0.0)
            G.affine_select(out=shiftdn[:], in_=shiftdn[:], compare_op=ALU.not_equal,
                            fill=1.0, base=-64, pattern=[[-1, 128]], channel_multiplier=1)
            G.memset(identbot[:], 0.0)
            G.affine_select(out=identbot[:], in_=identbot[:], compare_op=ALU.not_equal,
                            fill=1.0, base=0, pattern=[[-1, 128]], channel_multiplier=1)
            G.memset(identbot[0:64, :], 0.0)
            G.memset(m9r[:], 0.0)
            G.affine_select(out=m9r[:], in_=m9r[:], compare_op=ALU.not_equal,
                            fill=1.0, base=-1, pattern=[[0, 4], [-1, 8], [0, 16]],
                            channel_multiplier=1)
            G.memset(m9r[0:1, :], 1.0)
            G.memset(m9l[:], 0.0)
            G.affine_select(out=m9l[:], in_=m9l[:], compare_op=ALU.not_equal,
                            fill=MC, base=-1, pattern=[[-1, 8], [0, 16]],
                            channel_multiplier=1)
            return G.memset(m9l[0:1, :], -MC)
        S.op("gpsimd", g_masks, w=["ident", "shiftdn", "identbot", "m9"])
        S.op("vector", lambda: V.memset(ones_col[:], 1.0), w=["ones_col"])
        S.op("vector", lambda: V.memset(eps_ln[:], LN_EPS), w=["eps_ln"])
        S.op("vector", lambda: V.memset(eps_rms[:], RMS_EPS), w=["eps_rms"])
        def s_warm():
            for f in (AF.Identity, AF.Copy, AF.Square, AF.Exp, AF.Ln, AF.Lrelu):
                kw = dict(alpha=0.01) if f == AF.Lrelu else {}
                SC.activation(stats[0:1, 0:1], eps_ln[0:1, 0:1], f, **kw)
            return SC.copy(stats[0:1, 0:1], eps_ln[0:1, 0:1])
        S.op("scalar", s_warm, r=["eps_ln"], w=["warmup"])
        def v_warm():
            c0, c1 = stats[0:1, 2:3], stats[0:1, 3:4]
            V.tensor_copy(c0, eps_ln[0:1, 0:1])
            V.tensor_tensor(c1, c0, c0, ALU.add)
            V.tensor_tensor(c1, c0, c0, ALU.mult)
            V.tensor_tensor(c1, c0, c0, ALU.subtract)
            V.tensor_tensor(c1, c0, c0, ALU.max)
            V.tensor_scalar(c1, c0, 1.0, 2.0, ALU.mult, ALU.add)
            V.tensor_scalar(c1, c0, c0, c0, ALU.subtract, ALU.mult)
            V.tensor_reduce(c1, stats[0:1, 0:2], AX.X, ALU.add)
            V.tensor_reduce(c1, stats[0:1, 0:2], AX.X, ALU.max, negate=True)
            V.reciprocal(c1, c0)
            return V.scalar_tensor_tensor(c1, c0, 1.0, c0, ALU.mult, ALU.add)
        S.op("vector", v_warm, r=["eps_ln"], w=["vwarmup"])
        S.op("vector", lambda: V.memset(q_strip[:], 0.0), w=["q_strip"])
        S.op("vector", lambda: V.memset(v_strip[0:64, :], 0.0), w=["v_strip_z"])
        S.op("vector", lambda: V.memset(Wup_vpad[:], 0.0), w=["Wup_vpad"])
        S.op("vector", lambda: V.memset(dumpB[:], 0.0), w=["dumpB"])

        bcast_bf16(g1_m, "g1_m", dg1, D)
        bcast_bf16(g2_m, "g2_m", dg2, D)
        S.dma("gpsimd", lambda: G.dma_start(bq_rows[:], dbq), w=["bq_rows"])
        S.dma("gpsimd", lambda: G.dma_start(bdown_row[:], dbdown), w=["bdown_row"])
        bup3 = dbup.rearrange("o (h v d) -> o h v d", h=16, v=2, d=64)
        bupk3 = bupk_rows[:].rearrange("o (h v d) -> o h v d", h=16, v=2, d=64)
        bupv3 = bupv_rows[:].rearrange("o (h v d) -> o h v d", h=16, v=2, d=64)
        S.dma("gpsimd", lambda: G.dma_start(bupk3[:, :, 0, :], bup3[:, :, 0, :]), w=["bupk"])
        S.dma("gpsimd", lambda: G.dma_start(bupk3[:, :, 1, :], bup3[:, :, 0, :]), w=["bupk"])
        S.op("vector", lambda: V.memset(bupv_rows[:], 0.0), w=["bupv"])
        S.dma("gpsimd", lambda: G.dma_start(bupv3[:, :, 1, :], bup3[:, :, 1, :]), w=["bupv"])
        hilo_row(bo_hi, bo_lo, "bo_hi", "bo_lo", dbo, D)
        hilo_row(bg_hi, bg_lo, "bg_hi", "bg_lo", dbg, 8)

        def load_ffn_weights():
            S.dma("gpsimd", lambda: G.dma_start(Wq_sb[:].rearrange("p (c f) -> p c f", c=8), dWq.rearrange("(c p) f -> p c f", p=128)),
                  w=["Wq_sb"])
            S.dma("gpsimd", lambda: G.dma_start(Wdown_sb[:].rearrange("p (c f) -> p c f", c=8),
                  dWdown.rearrange("(c p) f -> p c f", p=128)), w=["Wdown_sb"])
            S.op("vector", lambda: V.memset(Wup_sb[64:128, :], 0.0), w=["Wup_sb"])
            kvsrc = dWup.rearrange("p (h v d) -> p h v d", h=16, v=2, d=64)
            for rep in range(2):
                dst = Wup_sb[0:64, 0:2048].rearrange("p (h v d) -> p h v d", h=16, v=2, d=64)
                S.dma("gpsimd", lambda dst=dst, rep=rep: G.dma_start(dst[:, :, rep, :], kvsrc[:, :, 0, :]),
                      w=["Wup_sb"])
            dstv = Wup_vpad[0:64, :].rearrange("p (h v d) -> p h v d", h=16, v=2, d=64)
            S.dma("gpsimd", lambda dstv=dstv: G.dma_start(dstv[:, :, 1, :], kvsrc[:, :, 1, :]),
                  w=["Wup_vpad"])
            S.dma("gpsimd", lambda: G.dma_start(Wo_sb[:].rearrange("p (c f) -> p c f", c=8), dWo.rearrange("(c p) f -> p c f", p=128)),
                  w=["Wo_sb"])
            S.dma("gpsimd", lambda: G.dma_start(Wg_sb[:].rearrange("p (c f) -> p c f", c=8), dWg.rearrange("(c p) f -> p c f", p=128)),
                  w=["Wg_sb"])

        cL_sum = [stcols(1), stcols(1)]; cL_mu = [stcols(1), stcols(1)]
        cL_v = [stcols(1), stcols(1)]; cL_t = [stcols(1), stcols(1)]
        cL_r = [stcols(1), stcols(1)]; c_r1 = [stcols(1), stcols(1)]
        c_r3 = [stcols(1), stcols(1)]; c_r3h = [stcols(1), stcols(1)]
        c_zg = [stcols(1), stcols(1)]; c_gt = [stcols(1), stcols(1)]
        c_mx = [stcols(4), stcols(4)]; c_z4 = [stcols(4), stcols(4)]
        c_rz = [stcols(4), stcols(4)]
        c_ymu = [stcols(1), stcols(1)]; c_yv = [stcols(1), stcols(1)]
        c_yr = [stcols(1), stcols(1)]
        c_pm = stcols(1); c_pz = stcols(1)
        c_nmr = [stcols(1), stcols(1)]

        def emit_lnstats(kx, xs, c_sumv, c_muv, c_vv, c_tv, c_rv, eps, dump=None):
            eps = (eps[0], eps[1]) if isinstance(eps, tuple) else eps
            """LayerNorm stats for slab src xs (key kx) -> NEGATIVE mu in c_muv, r in c_rv."""
            db, kdb = dump if dump is not None else (dumpA, "dumpA")
            S.op("vector", lambda: V.tensor_reduce(st(c_sumv), xs, AX.X, ALU.add),
                 r=[kx], w=[stk(c_sumv)])
            S.op("scalar", lambda: SC.activation(db[:], xs, AF.Square, accum_out=st(c_vv)),
                 r=[kx], w=[kdb, stk(c_vv)])
            S.op("scalar", lambda: SC.activation(st(c_muv), st(c_sumv), AF.Identity, scale=-1.0 / D),
                 r=[stk(c_sumv)], w=[stk(c_muv)])
            S.op("scalar", lambda: SC.activation(st(c_tv), st(c_muv), AF.Square),
                 r=[stk(c_muv)], w=[stk(c_tv)])
            S.op("vector", lambda: V.tensor_scalar(st(c_vv), st(c_vv), 1.0 / D, st(c_tv),
                 ALU.mult, ALU.subtract), r=[stk(c_vv), stk(c_tv)], w=[stk(c_vv)])
            S.op("scalar", lambda: SC.activation(st(c_vv), st(c_vv), AF.Ln, bias=eps[0][:]),
                 r=[stk(c_vv), eps[1]], w=[stk(c_vv)])
            S.op("scalar", lambda: SC.activation(st(c_rv), st(c_vv), AF.Exp, scale=-0.5),
                 r=[stk(c_vv)], w=[stk(c_rv)])

        def emit_rms(ksrc, src, c_rv, scale_after=None, dump=None):
            db, kdb = dump if dump is not None else (dumpA, "dumpA")
            S.op("scalar", lambda: SC.activation(db[:], src, AF.Square, accum_out=st(c_rv)),
                 r=[ksrc], w=[kdb, stk(c_rv)])
            S.op("scalar", lambda: SC.activation(st(c_rv), st(c_rv), AF.Ln, bias=eps_rms[:], scale=1.0 / D),
                 r=[stk(c_rv), "eps_rms"], w=[stk(c_rv)])
            S.op("scalar", lambda: SC.activation(st(c_rv), st(c_rv), AF.Exp, scale=-0.5),
                 r=[stk(c_rv)], w=[stk(c_rv)])

        def emit_ln_u(s, kx, xs, ku, us):
            sp = s % 2
            lsum, lmu, lv, lt, lr = (cL_sum[sp], cL_mu[sp], cL_v[sp], cL_t[sp], cL_r[sp])
            nmr = c_nmr[sp]
            emit_lnstats(kx, xs, lsum, lmu, lv, lt, lr, (eps_ln, 'eps_ln'),
                         dump=(dumpC, "dumpC"))
            S.op("vector", lambda lmu=lmu, lr=lr, nmr=nmr: V.tensor_scalar(
                st(nmr), st(lmu), 1.0, st(lr), ALU.mult, ALU.mult),
                r=[stk(lmu), stk(lr)], w=[stk(nmr)])
            S.op("scalar", lambda xs=xs, us=us, lr=lr, nmr=nmr: SC.activation(
                us, xs, AF.Identity, scale=st(lr), bias=st(nmr)),
                r=[kx, stk(lr), stk(nmr)], w=[ku])

        def emit_slab_head(s, fid, dbg_on):
            xs = x_res[:, D * s:D * s + D]
            uslot = s % 2
            sp = s % 2
            us = u_ring[:, D * uslot:D * uslot + D]
            kx, ku = f"x{s}", f"u{uslot}"
            r1c = c_r1[sp]
            emit_ln_u(s, kx, xs, ku, us)
            if dbg_on and s == 0:
                dbg_store("u", u_ring[:, 0:D], [ku])
            emit_rms(ku, us, r1c, dump=(dumpC, "dumpC"))
            hs = h_tok[:, D * (s % 2):D * (s % 2) + D]
            kh = f"h{s % 2}"
            S.op("scalar", lambda us=us, r1c=r1c: SC.activation(
                dumpC[:], us, AF.Identity, scale=st(r1c)), r=[ku, stk(r1c)], w=["dumpC"])
            S.op("vector", lambda hs=hs: V.tensor_tensor(hs, dumpC[:], g1_m[:], ALU.mult),
                 r=["dumpC", "g1_m"], w=[kh])
            if dbg_on and s == 0:
                dbg_store("h", h_tok[:, 0:D], [kh])

        def emit_slab_body(s, fid, dbg_on, feed=None):
            xs = x_res[:, D * s:D * s + D]
            uslot = s % 2
            sp = s % 2
            us = u_ring[:, D * uslot:D * uslot + D]
            kx, ku = f"x{s}", f"u{uslot}"
            r3c, r3hc, zgc, gtc = (c_r3[sp], c_r3h[sp], c_zg[sp], c_gt[sp])
            mxc, z4c, rzc = c_mx[sp], c_z4[sp], c_rz[sp]
            hs = h_tok[:, D * (s % 2):D * (s % 2) + D]
            kh = f"h{s % 2}"
            hfm = h_fm[:, D * (s % 2):D * (s % 2) + D]
            khf = f"hfm{s % 2}"
            for c in range(8):
                S.op("tensor", lambda hs=hs, c=c: TE.transpose(
                    psT2[:, 128 * c:128 * c + 128], hs[:, 128 * c:128 * c + 128], ident[:]),
                    r=[kh, "ident"], w=["psT2.0", "psT2.1"])
            S.op("vector", lambda hfm=hfm: V.tensor_copy(hfm, psT2[:]),
                 r=["psT2.0", "psT2.1"], w=[khf])
            for b, ps, kps in ((0, psQ1, "psQ1"), (1, psQ2, "psQ2")):
                for oc in range(4):
                    occ = 4 * b + oc
                    S.op("tensor", lambda ps=ps, oc=oc, occ=occ: TE.matmul(
                        ps[:, 128 * oc:128 * oc + 128], bq_rows[:, 128 * occ:128 * occ + 128],
                        ones_col[:], start=True, stop=False),
                        r=["bq_rows", "ones_col"], w=[kps])
                    for kc in range(8):
                        S.op("tensor", lambda ps=ps, oc=oc, occ=occ, kc=kc, hfm=hfm: TE.matmul(
                            ps[:, 128 * oc:128 * oc + 128],
                            Wq_sb[:, D * kc + 128 * occ:D * kc + 128 * occ + 128],
                            hfm[:, 128 * kc:128 * kc + 128], start=False, stop=(kc == 7)),
                            r=["Wq_sb", khf], w=[kps])
            S.op("tensor", lambda: TE.matmul(psS[0:64, 0:128], bdown_row[:],
                 ones_col[:], start=True, stop=False), r=["bdown_row", "ones_col"], w=["psS"])
            for kc in range(8):
                S.op("tensor", lambda kc=kc, hfm=hfm: TE.matmul(psS[0:64, 0:128],
                     Wdown_sb[:, 64 * kc:64 * kc + 64], hfm[:, 128 * kc:128 * kc + 128],
                     start=False, stop=(kc == 7)), r=["Wdown_sb", khf], w=["psS"])
            S.op("scalar", lambda: SC.copy(dumpB[0:64, 0:128], psS[0:64, 0:128]),
                 r=["psS"], w=["dumpB"])
            if dbg_on and s == 0:
                dbg_store("d0", dumpB[:, 0:128], ["dumpB"])
            kvv = k_strip[:].rearrange("p (t h) -> p h t", t=128, h=16)
            vvv = v_strip[:].rearrange("p (t h) -> p h t", t=128, h=16)
            for grp in range(4):
                ps, kps = (psK1, "psK1") if grp % 2 == 0 else (psK2, "psK2")
                for j in range(4):
                    h = 4 * grp + j
                    S.op("tensor", lambda ps=ps, j=j, h=h: TE.matmul(
                        ps[:, 128 * j:128 * j + 128], bupk_rows[:, 128 * h:128 * h + 128],
                        ones_col[:], start=True, stop=False), r=["bupk", "ones_col"], w=[kps])
                    S.op("tensor", lambda ps=ps, j=j, h=h: TE.matmul(
                        ps[:, 128 * j:128 * j + 128], Wup_sb[:, 128 * h:128 * h + 128],
                        dumpB[:, 0:128], start=False, stop=True),
                        r=["Wup_sb", "dumpB"], w=[kps])
                src = ps[:].rearrange("p (c t) -> p c t", c=4, t=128)
                for j in range(4):
                    S.op("vector", lambda src=src, grp=grp, j=j, kvv=kvv: V.tensor_copy(
                        kvv[:, 4 * grp + j, :], src[:, j, :]), r=[kps], w=["k_strip"])
            for grp in range(4):
                ps, kps = (psK1, "psK1") if grp % 2 == 0 else (psK2, "psK2")
                for j in range(4):
                    h = 4 * grp + j
                    S.op("tensor", lambda ps=ps, j=j, h=h: TE.matmul(
                        ps[:, 128 * j:128 * j + 128], bupv_rows[:, 128 * h:128 * h + 128],
                        ones_col[:], start=True, stop=False), r=["bupv", "ones_col"], w=[kps])
                    S.op("tensor", lambda ps=ps, j=j, h=h: TE.matmul(
                        ps[:, 128 * j:128 * j + 128], Wup_vpad[:, 128 * h:128 * h + 128],
                        dumpB[:, 0:128], start=False, stop=True),
                        r=["Wup_vpad", "dumpB"], w=[kps])
                src = ps[64:128, :].rearrange("p (c t) -> p c t", c=4, t=128)
                for j in range(4):
                    S.op("vector", lambda src=src, grp=grp, j=j, vvv=vvv: V.tensor_copy(
                        vvv[64:128, 4 * grp + j, :], src[:, j, :]), r=[kps], w=["v_strip"])
            qv = q_strip[:].rearrange("p (t h) -> p h t", t=128, h=16)
            for b, ps, kps in ((0, psQ1, "psQ1"), (1, psQ2, "psQ2")):
                src = ps[:].rearrange("p (c t) -> p c t", c=4, t=128)
                for oc in range(4):
                    occ = 4 * b + oc
                    S.op("scalar", lambda src=src, oc=oc, occ=occ, qv=qv: SC.copy(
                        qv[0:64, 2 * occ, :], src[0:64, oc, :]), r=[kps], w=["q_strip"])
                    S.op("scalar", lambda src=src, oc=oc, occ=occ, qv=qv: SC.copy(
                        qv[64:128, 2 * occ + 1, :], src[64:128, oc, :]), r=[kps], w=["q_strip"])
            if dbg_on and s == 0:
                dbg_store("q", q_strip[:], ["q_strip"])
                dbg_store("k", k_strip[:], ["k_strip"])
                dbg_store("v", v_strip[:], ["v_strip", "v_strip_z"])
            def blk_env(blk):
                bp = blk % 2
                return dict(
                    bp=bp,
                    pS=(psS if bp == 0 else psK1), kS=("psS" if bp == 0 else "psK1"),
                    pO=(psO if bp == 0 else psK2), kO=("psO" if bp == 0 else "psK2"),
                    Ebuf=(dumpA[:, 512:1024] if bp == 0 else dumpB[:, 512:1024]),
                    kE=("dumpA" if bp == 0 else "dumpB"),
                    Pbuf=pv_buf[:, 1024 * bp:1024 * bp + 512],
                    Vbuf=pv_buf[:, 1024 * bp + 512:1024 * bp + 1024],
                    kPV="pv%d" % bp, tcol=512 * bp,
                    kT1=f"psT1.{bp}", kT2=f"psT2.{bp}")

            def blk_A(blk):
                E = blk_env(blk)
                pS, kS, Ebuf, kE, Pbuf, kPV = E["pS"], E["kS"], E["Ebuf"], E["kE"], E["Pbuf"], E["kPV"]
                S.op("tensor", lambda pS=pS: TE.matmul(pS[:], m9l[:], m9r[:],
                     start=True, stop=False), r=["m9"], w=[kS])
                for j in range(4):
                    g = 4 * blk + j
                    S.op("tensor", lambda pS=pS, g=g, j=j: TE.matmul(
                        pS[:, 128 * j:128 * j + 128], q_strip[:, 128 * g:128 * g + 128],
                        k_strip[:, 128 * g:128 * g + 128], start=False, stop=True),
                        r=["q_strip", "k_strip"], w=[kS])
                S.op("vector", lambda pS=pS, mxc=mxc: V.tensor_reduce(
                    stats[:, mxc:mxc + 4], pS[:].rearrange("p (j t) -> p j t", j=4),
                    AX.X, ALU.max, negate=True), r=[kS], w=[stk(mxc)])
                S.op("scalar", lambda mxc=mxc: SC.activation(
                    stats[:, mxc:mxc + 4], stats[:, mxc:mxc + 4], AF.Identity, scale=0.125),
                    r=[stk(mxc)], w=[stk(mxc)])
                for j in range(4):
                    S.op("scalar", lambda pS=pS, Ebuf=Ebuf, j=j, mxc=mxc, z4c=z4c: SC.activation(
                        Ebuf[:, 128 * j:128 * j + 128], pS[:, 128 * j:128 * j + 128],
                        AF.Exp, scale=0.125, bias=st(mxc + j), accum_out=st(z4c + j)),
                        r=[kS, stk(mxc)], w=[kE, stk(z4c)])
                S.op("vector", lambda rzc=rzc, z4c=z4c: V.reciprocal(stats[:, rzc:rzc + 4],
                     stats[:, z4c:z4c + 4]), r=[stk(z4c)], w=[stk(rzc)])
                for j in range(4):
                    S.op("scalar", lambda Pbuf=Pbuf, Ebuf=Ebuf, j=j, rzc=rzc, kE=kE, kPV=kPV: SC.activation(
                        Pbuf[:, 128 * j:128 * j + 128], Ebuf[:, 128 * j:128 * j + 128],
                        AF.Identity, scale=st(rzc + j)), r=[kE, stk(rzc)], w=[kPV])

            def blk_B(blk):
                E = blk_env(blk)
                pO, kO, Pbuf, Vbuf, kPV, tcol, kT1, kT2 = (E["pO"], E["kO"], E["Pbuf"],
                    E["Vbuf"], E["kPV"], E["tcol"], E["kT1"], E["kT2"])
                for j in range(4):
                    g = 4 * blk + j
                    S.op("tensor", lambda Pbuf=Pbuf, j=j, tcol=tcol, kPV=kPV, kT1=kT1: TE.transpose(
                        psT1[:, tcol + 128 * j:tcol + 128 * j + 128],
                        Pbuf[:, 128 * j:128 * j + 128], ident[:]), r=[kPV, "ident"], w=[kT1])
                    S.op("tensor", lambda g=g, j=j, tcol=tcol, kT2=kT2: TE.transpose(
                        psT2[:, tcol + 128 * j:tcol + 128 * j + 128],
                        v_strip[:, 128 * g:128 * g + 128], ident[:]),
                        r=["v_strip", "v_strip_z", "ident"], w=[kT2])
                S.op("vector", lambda Pbuf=Pbuf, tcol=tcol, kT1=kT1, kPV=kPV: V.tensor_copy(
                     Pbuf[:], psT1[:, tcol:tcol + 512]), r=[kT1], w=[kPV])
                S.op("vector", lambda Vbuf=Vbuf, tcol=tcol, kT2=kT2, kPV=kPV: V.tensor_copy(
                     Vbuf[:], psT2[:, tcol:tcol + 512]), r=[kT2], w=[kPV])
                for j in range(4):
                    S.op("tensor", lambda pO=pO, Pbuf=Pbuf, Vbuf=Vbuf, j=j, kPV=kPV, kO=kO: TE.matmul(
                        pO[:, 128 * j:128 * j + 128], Vbuf[:, 128 * j:128 * j + 128],
                        Pbuf[:, 128 * j:128 * j + 128], start=True, stop=True),
                        r=[kPV], w=[kO])
                S.op("scalar", lambda pO=pO, blk=blk, kO=kO: SC.copy(
                    outSraw[:, 512 * blk:512 * blk + 512], pO[:]), r=[kO], w=["outSraw"])

            blk_A(0); blk_A(1); blk_B(0); blk_A(2); blk_B(1); blk_A(3); blk_B(2); blk_B(3)
            if dbg_on and s == 0:
                dbg_store("outs", outSraw[:], ["outSraw"])
            osv = outSraw[:].rearrange("p (t h) -> p h t", t=128, h=16)
            for pc in range(8):
                pp, kpp = (psS, "psS") if pc < 4 else (psO, "psO")
                col = 128 * (pc % 4)
                S.op("tensor", lambda pp=pp, pc=pc, col=col, osv=osv: TE.matmul(
                    pp[:, col:col + 128], shiftdn[:],
                    osv[:, 2 * pc, :], start=True, stop=False),
                    r=["outSraw", "shiftdn"], w=[kpp])
                S.op("tensor", lambda pp=pp, pc=pc, col=col, osv=osv: TE.matmul(
                    pp[:, col:col + 128], identbot[:],
                    osv[:, 2 * pc + 1, :], start=False, stop=True),
                    r=["outSraw", "identbot"], w=[kpp])
            S.op("vector", lambda: V.tensor_copy(ofmp[:, 0:512], psS[:]), r=["psS"], w=["ofmp"])
            S.op("vector", lambda: V.tensor_copy(ofmp[:, 512:1024], psO[:]), r=["psO"], w=["ofmp"])
            if dbg_on and s == 0:
                dbg_store("ofmp", ofmp[:], ["ofmp"])
            for fh, ps, kps in ((0, psQ1, "psQ1"), (1, psQ2, "psQ2")):
                S.op("tensor", lambda ps=ps, fh=fh: TE.matmul(
                    ps[:], ones_col[:], bo_hi[:, 512 * fh:512 * fh + 512],
                    start=True, stop=False), r=["bo_hi", "ones_col"], w=[kps])
                S.op("tensor", lambda ps=ps, fh=fh: TE.matmul(
                    ps[:], ones_col[:], bo_lo[:, 512 * fh:512 * fh + 512],
                    start=False, stop=False), r=["bo_lo", "ones_col"], w=[kps])
                for kc in range(8):
                    S.op("tensor", lambda ps=ps, fh=fh, kc=kc: TE.matmul(
                        ps[:], ofmp[:, 128 * kc:128 * kc + 128],
                        Wo_sb[:, D * kc + 512 * fh:D * kc + 512 * fh + 512],
                        start=False, stop=(kc == 7)), r=["ofmp", "Wo_sb"], w=[kps])
                S.op("vector", lambda ps=ps, us=us, fh=fh: V.scalar_tensor_tensor(
                    dumpB[:, 512 * fh:512 * fh + 512], ps[:], 1.0,
                    us[:, 512 * fh:512 * fh + 512], ALU.mult, ALU.add),
                    r=[kps, ku], w=["dumpB"])
                S.op("vector", lambda ps=ps, xs=xs, fh=fh: V.scalar_tensor_tensor(
                    xs[:, 512 * fh:512 * fh + 512], ps[:], 0.5,
                    xs[:, 512 * fh:512 * fh + 512], ALU.mult, ALU.add),
                    r=[kps, kx], w=[kx])
            if dbg_on and s == 0:
                dbg_store("z", dumpB[:], ["dumpB"])
            S.op("vector", lambda xs=xs, us=us: V.scalar_tensor_tensor(
                xs, us, 0.5, xs, ALU.mult, ALU.add), r=[ku, kx], w=[kx])
            emit_rms("dumpB", dumpB[:], r3c)
            S.op("scalar", lambda r3hc=r3hc, r3c=r3c: SC.activation(st(r3hc), st(r3c),
                 AF.Identity, scale=0.5), r=[stk(r3c)], w=[stk(r3hc)])
            S.op("vector", lambda: V.tensor_tensor(dumpB[:], dumpB[:], g2_m[:], ALU.mult),
                 r=["dumpB", "g2_m"], w=["dumpB"])
            zslab = zfm[:, D * (s % 8):D * (s % 8) + D]
            kzf = f"zfm{s % 8}"
            for c in range(8):
                S.op("tensor", lambda c=c: TE.transpose(
                    psT2[:, 128 * c:128 * c + 128], dumpB[:, 128 * c:128 * c + 128], ident[:]),
                    r=["dumpB", "ident"], w=["psT2.0", "psT2.1"])
            S.op("vector", lambda zslab=zslab: V.tensor_copy(zslab, psT2[:]),
                 r=["psT2.0", "psT2.1"], w=[kzf])
            if feed is not None:
                feed()
            S.op("tensor", lambda: TE.matmul(psO[:, 0:8], ones_col[:], bg_hi[:],
                 start=True, stop=False), r=["bg_hi", "ones_col"], w=["psO"])
            S.op("tensor", lambda: TE.matmul(psO[:, 0:8], ones_col[:], bg_lo[:],
                 start=False, stop=False), r=["bg_lo", "ones_col"], w=["psO"])
            for kc in range(8):
                S.op("tensor", lambda zslab=zslab, kc=kc, kzf=kzf: TE.matmul(
                    psO[:, 0:8], zslab[:, 128 * kc:128 * kc + 128],
                    Wg_sb[:, 8 * kc:8 * kc + 8], start=False, stop=(kc == 7)),
                    r=[kzf, "Wg_sb"], w=["psO"])
            S.op("scalar", lambda r3c=r3c: SC.activation(glog[:], psO[:, 0:8], AF.Identity,
                 scale=st(r3c)), r=["psO", stk(r3c)], w=["glog"])
            S.op("vector", lambda gtc=gtc: V.tensor_reduce(st(gtc), glog[:], AX.X, ALU.max,
                 negate=True), r=["glog"], w=[stk(gtc)])
            S.op("scalar", lambda gtc=gtc, zgc=zgc: SC.activation(glog[:], glog[:], AF.Exp,
                 bias=st(gtc), accum_out=st(zgc)), r=["glog", stk(gtc)], w=["glog", stk(zgc)])
            S.op("vector", lambda zgc=zgc: V.reciprocal(st(zgc), st(zgc)),
                 r=[stk(zgc)], w=[stk(zgc)])
            sgw = gw[:, 8 * (s % 8):8 * (s % 8) + 8]
            S.op("vector", lambda sgw=sgw, zgc=zgc, r3hc=r3hc: V.tensor_scalar(sgw, glog[:],
                 st(zgc), st(r3hc), ALU.mult, ALU.mult),
                 r=["glog", stk(zgc), stk(r3hc)], w=[f"gw{s % 8}"])

        def emit_expert_unit(e, fh, slabs):
            slot = (2 * e + fh) % 2
            wslot = We_sb[:, 4096 * slot:4096 * slot + 4096]
            kws = f"we{slot}"
            wsrc = dWe[e].rearrange("(c p) f -> p c f", p=128)[:, :, 512 * fh:512 * fh + 512]
            S.dma("gpsimd", lambda wslot=wslot, wsrc=wsrc: G.dma_start(
                wslot.rearrange("p (c f) -> p c f", c=8), wsrc), w=[kws])
            for s in slabs:
                ps, kps = (psK1, "psK1") if (s % 2 == 0) else (psK2, "psK2")
                zslab = zfm[:, D * (s % 8):D * (s % 8) + D]
                for kc in range(8):
                    S.op("tensor", lambda ps=ps, zslab=zslab, wslot=wslot, kc=kc, s=s: TE.matmul(
                        ps[:], zslab[:, 128 * kc:128 * kc + 128],
                        wslot[:, 512 * kc:512 * kc + 512],
                        start=(kc == 0), stop=(kc == 7)),
                        r=[f"zfm{s % 8}", kws], w=[kps])
                xsl = x_res[:, D * s + 512 * fh:D * s + 512 * fh + 512]
                S.op("vector", lambda ps=ps, xsl=xsl, s=s, e=e: V.scalar_tensor_tensor(
                    xsl, ps[:], gw[:, 8 * (s % 8) + e:8 * (s % 8) + e + 1], xsl,
                    ALU.mult, ALU.add), r=[kps, f"gw{s % 8}"], w=[f"x{s}"])

        def emit_ffn(fid, dbg_on):
            if fid == 1:
                S.op("vector", lambda: V.memset(q_strip[:], 0.0), w=["q_strip"])
                S.op("vector", lambda: V.memset(v_strip[0:64, :], 0.0), w=["v_strip_z", "v_strip"])
            mkfeed = lambda s: (lambda: emit_slab_head(s + 2, fid, dbg_on)) if s + 2 < NS else None
            emit_slab_head(0, fid, dbg_on)
            emit_slab_head(1, fid, dbg_on)
            for s in range(8):
                emit_slab_body(s, fid, dbg_on, feed=mkfeed(s))
            sl0 = list(range(8))
            for e in range(8):
                for fh in range(2):
                    emit_expert_unit(e, fh, sl0)
            for s in range(8, 16):
                emit_slab_body(s, fid, dbg_on, feed=mkfeed(s))
            if dbg_on:
                dbg_store("gate", gw[:], [f"gw{i}" for i in range(8)])
                dbg_store("zfm", zfm[:], [f"zfm{i}" for i in range(8)])
            sl1 = list(range(8, 16))
            return [(lambda e=e, fh=fh: emit_expert_unit(e, fh, sl1))
                    for e in range(8) for fh in range(2)]

        def emit_attn_part1(dbg_on, feeder=()):
            S.dma("gpsimd", lambda: G.dma_start(We_sb[:].rearrange("p (c f) -> p c f", c=8), dWr.rearrange("(c p) f -> p c f", p=128)),
                  w=["we0", "we1"])
            for c in range(8):
                for oc in range(8):
                    S.op("tensor", lambda c=c, oc=oc: TE.transpose(
                        psT1[:, 0:128], We_sb[:, D * oc + 128 * c:D * oc + 128 * c + 128], ident[:]),
                        r=["we0", "we1", "ident"], w=["psT1.0", "psT1.1"])
                    S.op("vector", lambda c=c, oc=oc: V.tensor_copy(
                        Wq_sb[:, D * c + 128 * oc:D * c + 128 * oc + 128], psT1[:, 0:128]),
                        r=["psT1.0", "psT1.1"], w=["Wq_sb"])
            S.dma("gpsimd", lambda: G.dma_start(We_sb[:].rearrange("p (c f) -> p c f", c=8), dWf.rearrange("(c p) f -> p c f", p=128)),
                  w=["we0", "we1"])
            for c in range(8):
                for oc in range(8):
                    S.op("tensor", lambda c=c, oc=oc: TE.transpose(
                        psT1[:, 0:128], We_sb[:, D * oc + 128 * c:D * oc + 128 * c + 128], ident[:]),
                        r=["we0", "we1", "ident"], w=["psT1.0", "psT1.1"])
                    S.op("vector", lambda c=c, oc=oc: V.tensor_copy(
                        Wo_sb[:, D * c + 128 * oc:D * c + 128 * oc + 128], psT1[:, 0:128]),
                        r=["psT1.0", "psT1.1"], w=["Wo_sb"])
            bcast_rows(lng_m, "q_strip", dlng)
            bcast_rows(lnb_m, "k_strip", dlnb)
            S.dma("sync", lambda: SY.dma_start(row_f[:, 0:128], dattr), w=["row_f"])
            S.op("vector", lambda: V.tensor_copy(row_hi[:, 0:128], row_f[:, 0:128]),
                 r=["row_f"], w=["row_hi"])
            for rep in range(8):
                S.op("tensor", lambda rep=rep: TE.matmul(psS[:, 0:128], ones_col[:],
                     row_hi[:, 0:128], start=True, stop=True), r=["row_hi", "ones_col"], w=["psS"])
                S.op("vector", lambda rep=rep: V.tensor_copy(
                    attr_m[:, 128 * rep:128 * rep + 128], psS[:, 0:128]), r=["psS"], w=["outSraw"])
            def a1_head(s):
                xs = x_res[:, D * s:D * s + D]
                us = u_ring[:, D * (s % 2):D * (s % 2) + D]
                kx, ku = f"x{s}", f"u{s % 2}"
                emit_ln_u(s, kx, xs, ku, us)
                if dbg_on and s == 0:
                    dbg_store("u2", u_ring[:, 0:D], [ku])
                hs = h_tok[:, D * (s % 2):D * (s % 2) + D]
                S.op("scalar", lambda us=us, hs=hs, s=s: SC.copy(hs, us),
                     r=[ku], w=[f"h{s % 2}"])

            a1_head(0)
            for s in range(NS):
                hs = h_tok[:, D * (s % 2):D * (s % 2) + D]
                kh = f"h{s % 2}"
                hfm = h_fm[:, D * (s % 2):D * (s % 2) + D]
                khf = f"hfm{s % 2}"
                for c in range(8):
                    S.op("tensor", lambda hs=hs, c=c: TE.transpose(
                        psT2[:, 128 * c:128 * c + 128], hs[:, 128 * c:128 * c + 128], ident[:]),
                        r=[kh, "ident"], w=["psT2.0", "psT2.1"])
                S.op("vector", lambda hfm=hfm: V.tensor_copy(hfm, psT2[:]), r=["psT2.0", "psT2.1"], w=[khf])
                for fh, ps, kps in ((0, psQ1, "psQ1"), (1, psQ2, "psQ2")):
                    for kc in range(8):
                        S.op("tensor", lambda ps=ps, hfm=hfm, fh=fh, kc=kc: TE.matmul(
                            ps[:], hfm[:, 128 * kc:128 * kc + 128],
                            Wq_sb[:, D * kc + 512 * fh:D * kc + 512 * fh + 512],
                            start=(kc == 0), stop=(kc == 7)), r=[khf, "Wq_sb"], w=[kps])
                S.op("scalar", lambda: SC.copy(ofmp[:, 0:512], psQ1[:]), r=["psQ1"], w=["ofmp"])
                S.op("scalar", lambda: SC.copy(ofmp[:, 512:1024], psQ2[:]), r=["psQ2"], w=["ofmp"])
                S.dma("sync", lambda s=s: SY.dma_start(FRS[:, 1024 * s:1024 * s + 1024], ofmp[:]),
                      r=["ofmp"])
                S.op("scalar", lambda: SC.activation(dumpA[:, 0:512], psQ1[:], AF.Lrelu, alpha=0.01),
                     r=["psQ1"], w=["dumpA"])
                S.op("scalar", lambda: SC.activation(dumpA[:, 512:1024], psQ2[:], AF.Lrelu, alpha=0.01),
                     r=["psQ2"], w=["dumpA"])
                S.op("vector", lambda: V.tensor_tensor(dumpB[:], dumpA[:], attr_m[:], ALU.mult),
                     r=["dumpA", "outSraw"], w=["dumpB"])
                S.op("vector", lambda: V.tensor_reduce(srtok_f[:], dumpB[:].rearrange(
                     "p (h d) -> p h d", h=8), AX.X, ALU.add), r=["dumpB"], w=["srtok_f"])
                S.op("scalar", lambda: SC.copy(srtok_bf[:], srtok_f[:]),
                     r=["srtok_f"], w=["srtok_bf"])
                if s < 8:
                    for u in feeder[2 * s:2 * s + 2]:
                        u()
                    a1_head(s + 1)
                    if s == 7:
                        a1_head(9)
                elif s + 2 < NS:
                    a1_head(s + 2)
                S.op("tensor", lambda: TE.transpose(psT1[0:8, 0:128], srtok_bf[:], ident[:]),
                     r=["srtok_bf", "ident"], w=["psT1.0", "psT1.1"])
                S.op("vector", lambda s=s: V.tensor_copy(sr_all[:, 128 * s:128 * s + 128],
                     psT1[0:8, 0:128]), r=["psT1.0", "psT1.1"], w=[f"sr{s}", "pv0", "pv1"])
            srk = [f"sr{s}" for s in range(NS)]
            if dbg_on:
                dbg_store("sr", sr_all[:], srk)
            S.op("vector", lambda: V.tensor_reduce(stats[0:8, c_pm:c_pm + 1], sr_all[:], AX.X,
                 ALU.max, negate=True), r=srk + ["pv0", "pv1"], w=[stk(c_pm)])
            S.op("scalar", lambda: SC.activation(sr_all[:], sr_all[:], AF.Exp,
                 bias=stats[0:8, c_pm:c_pm + 1], accum_out=stats[0:8, c_pz:c_pz + 1]),
                 r=[stk(c_pm)] + srk, w=[stk(c_pz)] + srk + ["pv0", "pv1"])
            S.op("vector", lambda: V.reciprocal(stats[0:8, c_pz:c_pz + 1],
                 stats[0:8, c_pz:c_pz + 1]), r=[stk(c_pz)], w=[stk(c_pz)])
            S.op("scalar", lambda: SC.activation(sr_all[:], sr_all[:], AF.Identity,
                 scale=stats[0:8, c_pz:c_pz + 1]), r=[stk(c_pz)] + srk, w=srk + ["pv0", "pv1"])
            if dbg_on:
                dbg_store("p", sr_all[:], srk)
            for s in range(NS):
                S.op("tensor", lambda s=s: TE.transpose(psT2[:, 8 * s:8 * s + 8],
                     sr_all[0:8, 128 * s:128 * s + 128], ident[0:8, 0:8]), r=[f"sr{s}", "pv0", "pv1", "ident"], w=["psT2.0", "psT2.1"])
            S.op("vector", lambda: V.tensor_copy(pT[:], psT2[:, 0:128]), r=["psT2.0", "psT2.1"], w=["pT"])
            for s in range(NS):
                slot = s % 2
                fbuf = h_fm[:, D * slot:D * slot + D]
                kf = f"hfm{slot}"
                S.dma("sync", lambda s=s, fbuf=fbuf: SY.dma_start(fbuf, FRS[:, 1024 * s:1024 * s + 1024]),
                      w=[kf])
                for h in range(8):
                    S.op("tensor", lambda h=h, s=s, fbuf=fbuf, kf=kf: TE.matmul(
                        psK1[:, h:h + 1], fbuf[:, 128 * h:128 * h + 128],
                        pT[:, 8 * s + h:8 * s + h + 1], start=(s == 0 and h == 0),
                        stop=(s == NS - 1)), r=[kf, "pT"], w=["psK1"])
            S.op("vector", lambda: V.tensor_copy(ctx_bf[:], psK1[:, 0:8]), r=["psK1"], w=["ctx_bf"])
            if dbg_on:
                dbg_store("ctx", ctx_bf[:], ["ctx_bf"])
            for oc in range(8):
                for c in range(8):
                    S.op("tensor", lambda oc=oc, c=c: TE.matmul(
                        psK2[:, oc:oc + 1], Wo_sb[:, D * c + 128 * oc:D * c + 128 * oc + 128],
                        ctx_bf[:, c:c + 1], start=(c == 0), stop=(c == 7)),
                        r=["Wo_sb", "ctx_bf"], w=["psK2"])
            S.op("vector", lambda: V.tensor_copy(fhFM_bf[:], psK2[:, 0:8]), r=["psK2"], w=["fhFM_bf"])
            S.op("tensor", lambda: TE.transpose(psT1[0:8, 0:128], fhFM_bf[:], ident[:]),
                 r=["fhFM_bf", "ident"], w=["psT1.0", "psT1.1"])
            S.op("vector", lambda: V.tensor_copy(fhT8[:], psT1[0:8, 0:128]), r=["psT1.0", "psT1.1"], w=["fhT8"])
            for c in range(8):
                S.dma("gpsimd", lambda c=c: G.dma_start(row_hi[:, 128 * c:128 * c + 128],
                      fhT8[c:c + 1, :]), r=["fhT8"], w=["row_hi"])
            for hh in range(0, D, 512):
                S.op("tensor", lambda hh=hh: TE.matmul(psS[:], ones_col[:],
                     row_hi[:, hh:hh + 512], start=True, stop=True),
                     r=["row_hi", "ones_col"], w=["psS"])
                S.op("vector", lambda hh=hh: V.tensor_copy(fh_m[:, hh:hh + 512], psS[:]),
                     r=["psS"], w=["outSraw"])
            if dbg_on:
                dbg_store("fh", fh_m[:], ["outSraw"])

        def emit_attn_part2(dbg_on):
            def p2_head(s):
                xs = x_res[:, D * s:D * s + D]
                us = u_ring[:, D * (s % 2):D * (s % 2) + D]
                emit_ln_u(s, f"x{s}", xs, f"u{s % 2}", us)

            p2_head(0)
            for s in range(NS):
                xs = x_res[:, D * s:D * s + D]
                uslot = s % 2
                sp = s % 2
                us = u_ring[:, D * uslot:D * uslot + D]
                kx, ku = f"x{s}", f"u{uslot}"
                lsum, lmu, lv, lt, lr = (cL_sum[sp], cL_mu[sp], cL_v[sp], cL_t[sp], cL_r[sp])
                ymuc, yrc = c_ymu[sp], c_yr[sp]
                if s + 1 < NS:
                    p2_head(s + 1)
                S.op("vector", lambda us=us: V.tensor_tensor(us, us, fh_m[:], ALU.add),
                     r=[ku, "outSraw"], w=[ku])
                S.op("vector", lambda us=us, lsum=lsum: V.tensor_reduce(st(lsum), us, AX.X, ALU.add),
                     r=[ku], w=[stk(lsum)])
                S.op("scalar", lambda us=us, lv=lv: SC.activation(dumpA[:], us, AF.Square,
                     accum_out=st(lv)), r=[ku], w=["dumpA", stk(lv)])
                S.op("scalar", lambda ymuc=ymuc, lsum=lsum: SC.activation(st(ymuc), st(lsum),
                     AF.Identity, scale=1.0 / D), r=[stk(lsum)], w=[stk(ymuc)])
                S.op("scalar", lambda lt=lt, ymuc=ymuc: SC.activation(st(lt), st(ymuc), AF.Square),
                     r=[stk(ymuc)], w=[stk(lt)])
                S.op("vector", lambda lv=lv, lt=lt: V.tensor_scalar(st(lv), st(lv), 1.0 / D, st(lt),
                     ALU.mult, ALU.subtract), r=[stk(lv), stk(lt)], w=[stk(lv)])
                S.op("scalar", lambda lv=lv: SC.activation(st(lv), st(lv), AF.Ln, bias=eps_ln[:]),
                     r=[stk(lv), "eps_ln"], w=[stk(lv)])
                S.op("scalar", lambda yrc=yrc, lv=lv: SC.activation(st(yrc), st(lv), AF.Exp, scale=-0.5),
                     r=[stk(lv)], w=[stk(yrc)])
                S.op("vector", lambda us=us, ymuc=ymuc, yrc=yrc: V.tensor_scalar(us, us, st(ymuc), st(yrc),
                     ALU.subtract, ALU.mult), r=[ku, stk(ymuc), stk(yrc)], w=[ku])
                S.op("gpsimd", lambda us=us: G.tensor_tensor(us, us, lng_m[:], ALU.mult),
                     r=[ku, "q_strip"], w=[ku])
                S.op("gpsimd", lambda us=us: G.tensor_tensor(us, us, lnb_m[:], ALU.add),
                     r=[ku, "k_strip"], w=[ku])
                S.op("vector", lambda xs=xs, us=us: V.tensor_tensor(xs, xs, us, ALU.add),
                     r=[ku, kx], w=[kx])
            if dbg_on:
                dbg_store("x2", x_res[:, 0:D], ["x0"])

        DBG = bool(debug)
        load_ffn_weights()
        dbg_store("wup", Wup_sb[:], ["Wup_sb"])
        dbg_store("wdn", Wdown_sb[:], ["Wdown_sb"])
        dbg_store("wq", Wq_sb[:], ["Wq_sb"])
        dbg_store("bupk", bupk_rows[:], ["bupk"])
        dbg_store("bupv", bupv_rows[:], ["bupv"])
        dbg_store("wvp", Wup_vpad[:], ["Wup_vpad"])
        pend1 = emit_ffn(0, DBG)
        if DBG:
            dbg_store("x1", x_res[:, 0:D], ["x0"])
        emit_attn_part1(DBG, feeder=pend1)
        load_ffn_weights()
        emit_attn_part2(DBG)
        pend2 = emit_ffn(1, False)
        for u in pend2:
            u()

        for s in range(NS):
            src = x_res[:, D * s:D * s + D]
            S.dma("sync", lambda s=s, src=src: SY.dma_start(OUT[128 * s:128 * s + 128, :], src),
                  r=[f"x{s}"])

        S.finalize(nc, ctx)
        block = ctx.enter_context(nc.Block())

        @block.tensor
        def _(te):
            S.emit_engine("tensor", te)

        @block.vector
        def _(ve):
            S.emit_engine("vector", ve)

        @block.scalar
        def _(se):
            S.emit_engine("scalar", se)

        @block.gpsimd
        def _(ge):
            S.emit_engine("gpsimd", ge)
            S.final_wait("gpsimd", ge)

        @block.sync
        def _(sy):
            S.emit_engine("sync", sy)
            S.final_wait("sync", sy)

    return nc


_NC_CACHE = {}


def _get_nc(debug=()):
    key = tuple(debug)
    if key not in _NC_CACHE:
        _NC_CACHE[key] = build_nc(debug)
    return _NC_CACHE[key]


def make_in_maps(inputs):
    x = np.ascontiguousarray(np.asarray(inputs["L0_attn_h"], dtype=np.float32))
    common = dict(
        Wq=np.ascontiguousarray(inputs["L0_ffn_Wq"], np.float32),
        bq=np.ascontiguousarray(np.asarray(inputs["L0_ffn_bq"], np.float32).reshape(1, -1)),
        Wdown=np.ascontiguousarray(inputs["L0_ffn_Wdown"], np.float32),
        bdown=np.ascontiguousarray(np.asarray(inputs["L0_ffn_bdown"], np.float32).reshape(1, -1)),
        Wup=np.ascontiguousarray(inputs["L0_ffn_Wup"], np.float32),
        bup=np.ascontiguousarray(np.asarray(inputs["L0_ffn_bup"], np.float32).reshape(1, -1)),
        Wo=np.ascontiguousarray(inputs["L0_ffn_Wo"], np.float32),
        bo=np.ascontiguousarray(np.asarray(inputs["L0_ffn_bo"], np.float32).reshape(1, -1)),
        Wg=np.ascontiguousarray(inputs["L0_ffn_Wg"], np.float32),
        bg=np.ascontiguousarray(np.asarray(inputs["L0_ffn_bg"], np.float32).reshape(1, -1)),
        We=np.ascontiguousarray(inputs["L0_ffn_We"], np.float32),
        gamma1=np.ascontiguousarray(np.asarray(inputs["L0_ffn_gamma1"], np.float32).reshape(1, -1)),
        gamma2=np.ascontiguousarray(np.asarray(inputs["L0_ffn_gamma2"], np.float32).reshape(1, -1)),
        Wr=np.ascontiguousarray(inputs["L0_attn_Wr"], np.float32),
        Wf=np.ascontiguousarray(inputs["L0_attn_Wf"], np.float32),
        att_r=np.ascontiguousarray(np.asarray(inputs["L0_attn_att_r"], np.float32).reshape(1, -1)),
        ln_gamma=np.ascontiguousarray(np.asarray(inputs["L0_attn_ln_gamma"], np.float32).reshape(1, -1)),
        ln_beta=np.ascontiguousarray(np.asarray(inputs["L0_attn_ln_beta"], np.float32).reshape(1, -1)),
    )
    return [dict(common, x=np.ascontiguousarray(x[b])) for b in range(x.shape[0])]


def kernel(**inputs):
    in_maps = make_in_maps(inputs)
    nc = _get_nc()
    res = run_bass_kernel_spmd(nc, in_maps, core_ids=list(range(8)))
    return np.stack([r["out"] for r in res.results], axis=0)


# version 17
# speedup vs baseline: 1.0122x; 1.0040x over previous
import numpy as np
import ml_dtypes
import concourse.bass as bass
import concourse.mybir as mybir
from concourse.bass_utils import run_bass_kernel_spmd

F32 = mybir.dt.float32
BF16 = mybir.dt.bfloat16
AF = mybir.ActivationFunctionType
ALU = mybir.AluOpType
AX = mybir.AxisListType

T = 2048
NS = 16
D = 1024
NEG = -1e30
MC = 1e30
LN_EPS = 1e-5
RMS_EPS = 1e-6

DRAIN_BEFORE_INC = False

ENGINES = ("tensor", "vector", "scalar", "gpsimd", "sync")


class Sched:
    def __init__(self):
        self.ops = {e: [] for e in ENGINES}
        self.nq = {"sync": 0, "gpsimd": 0}
        self.last_w = {}
        self.readers = {}

    def op(self, eng, fn, r=(), w=()):
        idx = len(self.ops[eng])
        ident = ("e", eng, idx)
        deps, raw = self._deps(r, w, ident)
        self.ops[eng].append(dict(fn=fn, deps=deps, raw=raw, inc=False, dma=None))
        return ident

    def dma(self, eng, fn, r=(), w=()):
        qidx = self.nq[eng]
        self.nq[eng] += 1
        ident = ("q", eng, qidx)
        deps, raw = self._deps(r, w, ident)
        self.ops[eng].append(dict(fn=fn, deps=deps, raw=raw, inc=False, dma=qidx))
        return ident

    def _deps(self, r, w, ident):
        deps = []
        raw = []
        for k in r:
            lw = self.last_w.get(k)
            if lw is not None:
                deps.append(lw)
                raw.append(lw)
            self.readers.setdefault(k, []).append(ident)
        for k in w:
            lw = self.last_w.get(k)
            if lw is not None:
                deps.append(lw)
            deps.extend(self.readers.get(k, ()))
            self.last_w[k] = ident
            self.readers[k] = []
        return deps, raw

    def finalize(self, nc, ctx):
        self.esem = {e: ctx.enter_context(nc.semaphore(f"sem_{e}")) for e in ENGINES}
        self.qsem = {e: ctx.enter_context(nc.semaphore(f"qsem_{e}")) for e in self.nq}
        for eng in ENGINES:
            for o in self.ops[eng]:
                for kind, seng, sidx in o["deps"]:
                    if kind == "e" and seng != eng:
                        self.ops[seng][sidx]["inc"] = True
        pref = {}
        for eng in ENGINES:
            c = 0
            pl = []
            for o in self.ops[eng]:
                if o["inc"]:
                    c += 1
                pl.append(c)
            pref[eng] = pl
        for eng in ENGINES:
            seen = {}
            for o in self.ops[eng]:
                waits = {}
                for kind, seng, sidx in o["deps"]:
                    if kind == "e":
                        if seng == eng:
                            continue
                        sem, val = ("e", seng), pref[seng][sidx]
                        if val == 0:
                            continue
                    else:
                        if seng == eng and o["dma"] is not None:
                            continue
                        sem, val = ("q", seng), 16 * (sidx + 1)
                    if waits.get(sem, 0) < val:
                        waits[sem] = val
                ww = []
                for sem, val in sorted(waits.items(), key=str):
                    if seen.get(sem, 0) >= val:
                        continue
                    seen[sem] = val
                    ww.append((sem, val))
                o["waits"] = ww

    def emit_engine(self, eng, engobj):
        drained_upto = -1
        for i, o in enumerate(self.ops[eng]):
            if eng == "vector":
                same = [sidx for kind, seng, sidx in o["raw"]
                        if kind == "e" and seng == eng]
                if same and max(same) > drained_upto:
                    engobj.drain()
                    drained_upto = i - 1
            for (sk, sname), val in o["waits"]:
                sem = self.esem[sname] if sk == "e" else self.qsem[sname]
                engobj.wait_ge(sem, val)
            ins = o["fn"]()
            if o["dma"] is not None:
                ins.then_inc(self.qsem[eng], 16)
            elif o["inc"]:
                if eng == "vector" and DRAIN_BEFORE_INC:
                    ins = engobj.drain()
                    drained_upto = i
                ins.then_inc(self.esem[eng], 1)

    def final_wait(self, eng, engobj):
        if self.nq.get(eng, 0):
            engobj.wait_ge(self.qsem[eng], 16 * self.nq[eng])


def build_nc(debug=()):
    nc = bass.Bass("TRN2", target_bir_lowering=False)
    S = Sched()

    X = nc.dram_tensor("x", [T, D], F32, kind="ExternalInput").ap()
    dWq = nc.dram_tensor("Wq", [D, D], F32, kind="ExternalInput").ap()
    dbq = nc.dram_tensor("bq", [1, D], F32, kind="ExternalInput").ap()
    dWdown = nc.dram_tensor("Wdown", [D, 64], F32, kind="ExternalInput").ap()
    dbdown = nc.dram_tensor("bdown", [1, 64], F32, kind="ExternalInput").ap()
    dWup = nc.dram_tensor("Wup", [64, 2 * D], F32, kind="ExternalInput").ap()
    dbup = nc.dram_tensor("bup", [1, 2 * D], F32, kind="ExternalInput").ap()
    dWo = nc.dram_tensor("Wo", [D, D], F32, kind="ExternalInput").ap()
    dbo = nc.dram_tensor("bo", [1, D], F32, kind="ExternalInput").ap()
    dWg = nc.dram_tensor("Wg", [D, 8], F32, kind="ExternalInput").ap()
    dbg = nc.dram_tensor("bg", [1, 8], F32, kind="ExternalInput").ap()
    dWe = nc.dram_tensor("We", [8, D, D], F32, kind="ExternalInput").ap()
    dg1 = nc.dram_tensor("gamma1", [1, D], F32, kind="ExternalInput").ap()
    dg2 = nc.dram_tensor("gamma2", [1, D], F32, kind="ExternalInput").ap()
    dWr = nc.dram_tensor("Wr", [D, D], F32, kind="ExternalInput").ap()
    dWf = nc.dram_tensor("Wf", [D, D], F32, kind="ExternalInput").ap()
    dattr = nc.dram_tensor("att_r", [1, 128], F32, kind="ExternalInput").ap()
    dlng = nc.dram_tensor("ln_gamma", [1, D], F32, kind="ExternalInput").ap()
    dlnb = nc.dram_tensor("ln_beta", [1, D], F32, kind="ExternalInput").ap()
    FRS = nc.dram_tensor("fr_scratch", [128, NS * 1024], BF16).ap()
    OUT = nc.dram_tensor("out", [T, D], F32, kind="ExternalOutput").ap()
    dbgout = {}
    for name in debug:
        shape = {"u": [128, D], "h": [128, D], "q": [128, 2048], "k": [128, 2048],
                 "v": [128, 2048], "outs": [128, 2048], "z": [128, D],
                 "x1": [128, D], "x2": [128, D], "gate": [128, 64],
                 "u2": [128, D], "sr": [8, T], "p": [8, T],
                 "ctx": [128, 8], "fh": [128, D], "ofmp": [128, D],
                 "wup": [128, 2048], "wdn": [128, 512], "wq": [128, 8 * D],
                 "bupk": [1, 2048], "bupv": [1, 2048], "wvp": [128, 2048],
                 "d0": [128, 128], "wdnf": [128, 512], "zfm": [128, 8192]}[name]
        dbgout[name] = nc.dram_tensor("dbg_" + name, shape, F32, kind="ExternalOutput").ap()

    from contextlib import ExitStack
    ctx = ExitStack()
    with ctx:
        sbt = lambda n, s, d=F32: ctx.enter_context(nc.sbuf_tensor(n, s, d))
        x_res = sbt("x_res", [128, NS * D])
        Wq_sb = sbt("Wq_sb", [128, 8 * D], BF16)
        Wo_sb = sbt("Wo_sb", [128, 8 * D], BF16)
        We_sb = sbt("We_sb", [128, 8192], BF16)
        Wdown_sb = sbt("Wdown_sb", [128, 8 * 64], BF16)
        Wup_sb = sbt("Wup_sb", [128, 2048], BF16)
        Wup_vpad = sbt("Wup_vpad", [128, 2048], BF16)
        Wg_sb = sbt("Wg_sb", [128, 64], BF16)
        g1_m = sbt("g1_m", [128, D], BF16)
        g2_m = sbt("g2_m", [128, D], BF16)
        bq_rows = sbt("bq_rows", [1, 1024], BF16)
        bdown_row = sbt("bdown_row", [1, 64], BF16)
        bupk_rows = sbt("bupk_rows", [1, 2048], BF16)
        bupv_rows = sbt("bupv_rows", [1, 2048], BF16)
        bo_hi = sbt("bo_hi", [1, D], BF16)
        bo_lo = sbt("bo_lo", [1, D], BF16)
        bg_hi = sbt("bg_hi", [1, 8], BF16)
        bg_lo = sbt("bg_lo", [1, 8], BF16)
        row_f = sbt("row_f", [1, 512])
        row_hi = sbt("row_hi", [1, D], BF16)
        ones_col = sbt("ones_col", [1, 128], BF16)
        ident = sbt("ident", [128, 128], BF16)
        shiftdn = sbt("shiftdn", [128, 128], BF16)
        identbot = sbt("identbot", [128, 128], BF16)
        m9l = sbt("m9l", [9, 128], BF16)
        m9r = sbt("m9r", [9, 512], BF16)
        u_ring = sbt("u_ring", [128, 2 * D])
        h_tok = sbt("h_tok", [128, 2 * D], BF16)
        h_fm = sbt("h_fm", [128, 2 * D], BF16)
        zfm = sbt("zfm", [128, 8 * D], BF16)
        q_strip = sbt("q_strip", [128, 2048], BF16)
        k_strip = sbt("k_strip", [128, 2048], BF16)
        v_strip = sbt("v_strip", [128, 2048], BF16)
        outSraw = sbt("outSraw", [128, 2048], BF16)
        ofmp = sbt("ofmp", [128, D], BF16)
        dumpA = sbt("dumpA", [128, D], BF16)
        dumpB = sbt("dumpB", [128, D], BF16)
        dumpC = sbt("dumpC", [128, D], BF16)
        pv_buf = sbt("pv_buf", [128, 2 * D], BF16)
        sr_all = pv_buf[0:8, 0:T]
        stats = sbt("stats", [128, 64])
        gw = sbt("gw", [128, 64])
        glog = sbt("glog", [128, 8])
        pT = sbt("pT", [128, 128], BF16)
        ctx_bf = sbt("ctx_bf", [128, 8], BF16)
        fhFM_bf = sbt("fhFM_bf", [128, 8], BF16)
        fhT8 = sbt("fhT8", [8, 128], BF16)
        srtok_bf = sbt("srtok_bf", [128, 8], BF16)
        srtok_f = sbt("srtok_f", [128, 8])
        eps_ln = sbt("eps_ln", [128, 1])
        eps_rms = sbt("eps_rms", [128, 1])

        psQ1 = ctx.enter_context(nc.psum_tensor("psQ1", [128, 512], F32))
        psQ2 = ctx.enter_context(nc.psum_tensor("psQ2", [128, 512], F32))
        psK1 = ctx.enter_context(nc.psum_tensor("psK1", [128, 512], F32))
        psK2 = ctx.enter_context(nc.psum_tensor("psK2", [128, 512], F32))
        psS = ctx.enter_context(nc.psum_tensor("psS", [128, 512], F32))
        psO = ctx.enter_context(nc.psum_tensor("psO", [128, 512], F32))
        psT1 = ctx.enter_context(nc.psum_tensor("psT1", [128, 1024], BF16))
        psT2 = ctx.enter_context(nc.psum_tensor("psT2", [128, 1024], BF16))

        lng_m = q_strip[:].bitcast(F32)
        lnb_m = k_strip[:].bitcast(F32)
        fh_m = outSraw[:, 0:1024]
        attr_m = outSraw[:, 1024:2048]

        V, SC, TE, G, SY = nc.vector, nc.scalar, nc.tensor, nc.gpsimd, nc.sync

        stat_alloc = [0]

        def stcols(n):
            c = stat_alloc[0]
            stat_alloc[0] += n
            assert stat_alloc[0] <= 64, "stats overflow"
            return c

        def st(c):
            return stats[:, c:c + 1]

        def stk(c):
            return f"stc{c}"

        def dbg_store(name, src_ap, rkeys):
            if name in dbgout:
                S.dma("gpsimd", lambda: G.dma_start(dbgout[name], src_ap), r=list(rkeys))

        def bcast_rows(dst_sb, dst_key, src_dram, n=D):
            for hh in range(0, n, 512):
                nn = min(512, n - hh)
                S.dma("sync", lambda hh=hh, nn=nn: SY.dma_start(row_f[:, 0:nn],
                      src_dram[:, hh:hh + nn]), w=["row_f"])
                S.op("vector", lambda hh=hh, nn=nn: V.tensor_copy(row_hi[:, hh:hh + nn],
                     row_f[:, 0:nn]), r=["row_f"], w=["row_hi"])
                S.op("tensor", lambda hh=hh, nn=nn: TE.matmul(psS[:, 0:nn], ones_col[:],
                     row_hi[:, hh:hh + nn], start=True, stop=False),
                     r=["row_hi", "ones_col"], w=["psS"])
                S.op("vector", lambda hh=hh, nn=nn: V.tensor_tensor(row_f[:, 0:nn], row_f[:, 0:nn],
                     row_hi[:, hh:hh + nn], ALU.subtract), r=["row_hi"], w=["row_f"])
                S.op("vector", lambda hh=hh, nn=nn: V.tensor_copy(row_hi[:, hh:hh + nn],
                     row_f[:, 0:nn]), r=["row_f"], w=["row_hi"])
                S.op("tensor", lambda hh=hh, nn=nn: TE.matmul(psS[:, 0:nn], ones_col[:],
                     row_hi[:, hh:hh + nn], start=False, stop=True),
                     r=["row_hi", "ones_col"], w=["psS"])
                S.op("vector", lambda hh=hh, nn=nn: V.tensor_copy(dst_sb[:, hh:hh + nn], psS[:, 0:nn]),
                     r=["psS"], w=[dst_key])

        def bcast_bf16(dst_sb, dst_key, src_dram, n=D):
            for hh in range(0, n, 512):
                nn = min(512, n - hh)
                S.dma("sync", lambda hh=hh, nn=nn: SY.dma_start(row_f[:, 0:nn],
                      src_dram[:, hh:hh + nn]), w=["row_f"])
                S.op("vector", lambda hh=hh, nn=nn: V.tensor_copy(row_hi[:, hh:hh + nn],
                     row_f[:, 0:nn]), r=["row_f"], w=["row_hi"])
            for hh in range(0, n, 512):
                nn = min(512, n - hh)
                S.op("tensor", lambda hh=hh, nn=nn: TE.matmul(psS[:, 0:nn], ones_col[:],
                     row_hi[:, hh:hh + nn], start=True, stop=True),
                     r=["row_hi", "ones_col"], w=["psS"])
                S.op("vector", lambda hh=hh, nn=nn: V.tensor_copy(dst_sb[:, hh:hh + nn], psS[:, 0:nn]),
                     r=["psS"], w=[dst_key])

        def hilo_row(dst_hi, dst_lo, khi, klo, src_dram, n):
            for hh in range(0, n, 512):
                nn = min(512, n - hh)
                S.dma("sync", lambda hh=hh, nn=nn: SY.dma_start(row_f[:, 0:nn],
                      src_dram[:, hh:hh + nn]), w=["row_f"])
                S.op("vector", lambda hh=hh, nn=nn: V.tensor_copy(dst_hi[:, hh:hh + nn],
                     row_f[:, 0:nn]), r=["row_f"], w=[khi])
                S.op("vector", lambda hh=hh, nn=nn: V.tensor_tensor(row_f[:, 0:nn], row_f[:, 0:nn],
                     dst_hi[:, hh:hh + nn], ALU.subtract), r=[khi], w=["row_f"])
                S.op("vector", lambda hh=hh, nn=nn: V.tensor_copy(dst_lo[:, hh:hh + nn],
                     row_f[:, 0:nn]), r=["row_f"], w=[klo])

        Xv = X.rearrange("(s p) f -> p s f", p=128)
        S.dma("sync", lambda: SY.dma_start(x_res[:].rearrange("p (s f) -> p s f", s=NS), Xv),
              w=[f"x{s}" for s in range(NS)])

        def g_masks():
            G.memset(ident[:], 0.0)
            G.affine_select(out=ident[:], in_=ident[:], compare_op=ALU.not_equal,
                            fill=1.0, base=0, pattern=[[-1, 128]], channel_multiplier=1)
            G.memset(shiftdn[:], 0.0)
            G.affine_select(out=shiftdn[:], in_=shiftdn[:], compare_op=ALU.not_equal,
                            fill=1.0, base=-64, pattern=[[-1, 128]], channel_multiplier=1)
            G.memset(identbot[:], 0.0)
            G.affine_select(out=identbot[:], in_=identbot[:], compare_op=ALU.not_equal,
                            fill=1.0, base=0, pattern=[[-1, 128]], channel_multiplier=1)
            G.memset(identbot[0:64, :], 0.0)
            G.memset(m9r[:], 0.0)
            G.affine_select(out=m9r[:], in_=m9r[:], compare_op=ALU.not_equal,
                            fill=1.0, base=-1, pattern=[[0, 4], [-1, 8], [0, 16]],
                            channel_multiplier=1)
            G.memset(m9r[0:1, :], 1.0)
            G.memset(m9l[:], 0.0)
            G.affine_select(out=m9l[:], in_=m9l[:], compare_op=ALU.not_equal,
                            fill=MC, base=-1, pattern=[[-1, 8], [0, 16]],
                            channel_multiplier=1)
            return G.memset(m9l[0:1, :], -MC)
        S.op("gpsimd", g_masks, w=["ident", "shiftdn", "identbot", "m9"])
        S.op("vector", lambda: V.memset(ones_col[:], 1.0), w=["ones_col"])
        S.op("vector", lambda: V.memset(eps_ln[:], LN_EPS), w=["eps_ln"])
        S.op("vector", lambda: V.memset(eps_rms[:], RMS_EPS), w=["eps_rms"])
        def s_warm():
            for f in (AF.Identity, AF.Copy, AF.Square, AF.Exp, AF.Ln, AF.Lrelu):
                kw = dict(alpha=0.01) if f == AF.Lrelu else {}
                SC.activation(stats[0:1, 0:1], eps_ln[0:1, 0:1], f, **kw)
            return SC.copy(stats[0:1, 0:1], eps_ln[0:1, 0:1])
        S.op("scalar", s_warm, r=["eps_ln"], w=["warmup"])
        def v_warm():
            c0, c1 = stats[0:1, 2:3], stats[0:1, 3:4]
            V.tensor_copy(c0, eps_ln[0:1, 0:1])
            V.tensor_tensor(c1, c0, c0, ALU.add)
            V.tensor_tensor(c1, c0, c0, ALU.mult)
            V.tensor_tensor(c1, c0, c0, ALU.subtract)
            V.tensor_tensor(c1, c0, c0, ALU.max)
            V.tensor_scalar(c1, c0, 1.0, 2.0, ALU.mult, ALU.add)
            V.tensor_scalar(c1, c0, c0, c0, ALU.subtract, ALU.mult)
            V.tensor_reduce(c1, stats[0:1, 0:2], AX.X, ALU.add)
            V.tensor_reduce(c1, stats[0:1, 0:2], AX.X, ALU.max, negate=True)
            V.reciprocal(c1, c0)
            return V.scalar_tensor_tensor(c1, c0, 1.0, c0, ALU.mult, ALU.add)
        S.op("vector", v_warm, r=["eps_ln"], w=["vwarmup"])
        S.op("vector", lambda: V.memset(q_strip[:], 0.0), w=["q_strip"])
        S.op("vector", lambda: V.memset(v_strip[0:64, :], 0.0), w=["v_strip_z"])
        S.op("vector", lambda: V.memset(Wup_vpad[:], 0.0), w=["Wup_vpad"])
        S.op("vector", lambda: V.memset(dumpB[:], 0.0), w=["dumpB"])

        bcast_bf16(g1_m, "g1_m", dg1, D)
        bcast_bf16(g2_m, "g2_m", dg2, D)
        S.dma("gpsimd", lambda: G.dma_start(bq_rows[:], dbq), w=["bq_rows"])
        S.dma("gpsimd", lambda: G.dma_start(bdown_row[:], dbdown), w=["bdown_row"])
        bup3 = dbup.rearrange("o (h v d) -> o h v d", h=16, v=2, d=64)
        bupk3 = bupk_rows[:].rearrange("o (h v d) -> o h v d", h=16, v=2, d=64)
        bupv3 = bupv_rows[:].rearrange("o (h v d) -> o h v d", h=16, v=2, d=64)
        S.dma("gpsimd", lambda: G.dma_start(bupk3[:, :, 0, :], bup3[:, :, 0, :]), w=["bupk"])
        S.dma("gpsimd", lambda: G.dma_start(bupk3[:, :, 1, :], bup3[:, :, 0, :]), w=["bupk"])
        S.op("vector", lambda: V.memset(bupv_rows[:], 0.0), w=["bupv"])
        S.dma("gpsimd", lambda: G.dma_start(bupv3[:, :, 1, :], bup3[:, :, 1, :]), w=["bupv"])
        hilo_row(bo_hi, bo_lo, "bo_hi", "bo_lo", dbo, D)
        hilo_row(bg_hi, bg_lo, "bg_hi", "bg_lo", dbg, 8)

        def load_ffn_weights():
            S.dma("gpsimd", lambda: G.dma_start(Wq_sb[:].rearrange("p (c f) -> p c f", c=8), dWq.rearrange("(c p) f -> p c f", p=128)),
                  w=["Wq_sb"])
            S.dma("gpsimd", lambda: G.dma_start(Wdown_sb[:].rearrange("p (c f) -> p c f", c=8),
                  dWdown.rearrange("(c p) f -> p c f", p=128)), w=["Wdown_sb"])
            S.op("vector", lambda: V.memset(Wup_sb[64:128, :], 0.0), w=["Wup_sb"])
            kvsrc = dWup.rearrange("p (h v d) -> p h v d", h=16, v=2, d=64)
            for rep in range(2):
                dst = Wup_sb[0:64, 0:2048].rearrange("p (h v d) -> p h v d", h=16, v=2, d=64)
                S.dma("gpsimd", lambda dst=dst, rep=rep: G.dma_start(dst[:, :, rep, :], kvsrc[:, :, 0, :]),
                      w=["Wup_sb"])
            dstv = Wup_vpad[0:64, :].rearrange("p (h v d) -> p h v d", h=16, v=2, d=64)
            S.dma("gpsimd", lambda dstv=dstv: G.dma_start(dstv[:, :, 1, :], kvsrc[:, :, 1, :]),
                  w=["Wup_vpad"])
            S.dma("gpsimd", lambda: G.dma_start(Wo_sb[:].rearrange("p (c f) -> p c f", c=8), dWo.rearrange("(c p) f -> p c f", p=128)),
                  w=["Wo_sb"])
            S.dma("gpsimd", lambda: G.dma_start(Wg_sb[:].rearrange("p (c f) -> p c f", c=8), dWg.rearrange("(c p) f -> p c f", p=128)),
                  w=["Wg_sb"])

        cL_sum = [stcols(1), stcols(1)]; cL_mu = [stcols(1), stcols(1)]
        cL_v = [stcols(1), stcols(1)]; cL_t = [stcols(1), stcols(1)]
        cL_r = [stcols(1), stcols(1)]; c_r1 = [stcols(1), stcols(1)]
        c_r3 = [stcols(1), stcols(1)]; c_r3h = [stcols(1), stcols(1)]
        c_zg = [stcols(1), stcols(1)]; c_gt = [stcols(1), stcols(1)]
        c_mx = [stcols(4), stcols(4)]; c_z4 = [stcols(4), stcols(4)]
        c_rz = [stcols(4), stcols(4)]
        c_ymu = [stcols(1), stcols(1)]; c_yv = [stcols(1), stcols(1)]
        c_yr = [stcols(1), stcols(1)]
        c_pm = stcols(1); c_pz = stcols(1)
        c_nmr = [stcols(1), stcols(1)]

        def emit_lnstats(kx, xs, c_sumv, c_muv, c_vv, c_tv, c_rv, eps, dump=None):
            eps = (eps[0], eps[1]) if isinstance(eps, tuple) else eps
            """LayerNorm stats for slab src xs (key kx) -> NEGATIVE mu in c_muv, r in c_rv."""
            db, kdb = dump if dump is not None else (dumpA, "dumpA")
            S.op("vector", lambda: V.tensor_reduce(st(c_sumv), xs, AX.X, ALU.add),
                 r=[kx], w=[stk(c_sumv)])
            S.op("scalar", lambda: SC.activation(db[:], xs, AF.Square, accum_out=st(c_vv)),
                 r=[kx], w=[kdb, stk(c_vv)])
            S.op("scalar", lambda: SC.activation(st(c_muv), st(c_sumv), AF.Identity, scale=-1.0 / D),
                 r=[stk(c_sumv)], w=[stk(c_muv)])
            S.op("scalar", lambda: SC.activation(st(c_tv), st(c_muv), AF.Square),
                 r=[stk(c_muv)], w=[stk(c_tv)])
            S.op("vector", lambda: V.tensor_scalar(st(c_vv), st(c_vv), 1.0 / D, st(c_tv),
                 ALU.mult, ALU.subtract), r=[stk(c_vv), stk(c_tv)], w=[stk(c_vv)])
            S.op("scalar", lambda: SC.activation(st(c_vv), st(c_vv), AF.Ln, bias=eps[0][:]),
                 r=[stk(c_vv), eps[1]], w=[stk(c_vv)])
            S.op("scalar", lambda: SC.activation(st(c_rv), st(c_vv), AF.Exp, scale=-0.5),
                 r=[stk(c_vv)], w=[stk(c_rv)])

        def emit_rms(ksrc, src, c_rv, scale_after=None, dump=None):
            db, kdb = dump if dump is not None else (dumpA, "dumpA")
            S.op("scalar", lambda: SC.activation(db[:], src, AF.Square, accum_out=st(c_rv)),
                 r=[ksrc], w=[kdb, stk(c_rv)])
            S.op("scalar", lambda: SC.activation(st(c_rv), st(c_rv), AF.Ln, bias=eps_rms[:], scale=1.0 / D),
                 r=[stk(c_rv), "eps_rms"], w=[stk(c_rv)])
            S.op("scalar", lambda: SC.activation(st(c_rv), st(c_rv), AF.Exp, scale=-0.5),
                 r=[stk(c_rv)], w=[stk(c_rv)])

        def emit_ln_u(s, kx, xs, ku, us):
            sp = s % 2
            lsum, lmu, lv, lt, lr = (cL_sum[sp], cL_mu[sp], cL_v[sp], cL_t[sp], cL_r[sp])
            nmr = c_nmr[sp]
            emit_lnstats(kx, xs, lsum, lmu, lv, lt, lr, (eps_ln, 'eps_ln'),
                         dump=(dumpC, "dumpC"))
            S.op("gpsimd", lambda lmu=lmu, lr=lr, nmr=nmr: G.tensor_tensor(
                st(nmr), st(lmu), st(lr), ALU.mult),
                r=[stk(lmu), stk(lr)], w=[stk(nmr)])
            S.op("scalar", lambda xs=xs, us=us, lr=lr, nmr=nmr: SC.activation(
                us, xs, AF.Identity, scale=st(lr), bias=st(nmr)),
                r=[kx, stk(lr), stk(nmr)], w=[ku])

        def emit_slab_head(s, fid, dbg_on):
            xs = x_res[:, D * s:D * s + D]
            uslot = s % 2
            sp = s % 2
            us = u_ring[:, D * uslot:D * uslot + D]
            kx, ku = f"x{s}", f"u{uslot}"
            r1c = c_r1[sp]
            emit_ln_u(s, kx, xs, ku, us)
            if dbg_on and s == 0:
                dbg_store("u", u_ring[:, 0:D], [ku])
            emit_rms(ku, us, r1c, dump=(dumpC, "dumpC"))
            hs = h_tok[:, D * (s % 2):D * (s % 2) + D]
            kh = f"h{s % 2}"
            S.op("scalar", lambda us=us, r1c=r1c: SC.activation(
                dumpC[:], us, AF.Identity, scale=st(r1c)), r=[ku, stk(r1c)], w=["dumpC"])
            S.op("gpsimd", lambda hs=hs: G.tensor_tensor(hs, dumpC[:], g1_m[:], ALU.mult),
                 r=["dumpC", "g1_m"], w=[kh])
            if dbg_on and s == 0:
                dbg_store("h", h_tok[:, 0:D], [kh])

        def emit_slab_body(s, fid, dbg_on, feed=None):
            xs = x_res[:, D * s:D * s + D]
            uslot = s % 2
            sp = s % 2
            us = u_ring[:, D * uslot:D * uslot + D]
            kx, ku = f"x{s}", f"u{uslot}"
            r3c, r3hc, zgc, gtc = (c_r3[sp], c_r3h[sp], c_zg[sp], c_gt[sp])
            mxc, z4c, rzc = c_mx[sp], c_z4[sp], c_rz[sp]
            hs = h_tok[:, D * (s % 2):D * (s % 2) + D]
            kh = f"h{s % 2}"
            hfm = h_fm[:, D * (s % 2):D * (s % 2) + D]
            khf = f"hfm{s % 2}"
            for c in range(8):
                S.op("tensor", lambda hs=hs, c=c: TE.transpose(
                    psT2[:, 128 * c:128 * c + 128], hs[:, 128 * c:128 * c + 128], ident[:]),
                    r=[kh, "ident"], w=["psT2.0", "psT2.1"])
            S.op("vector", lambda hfm=hfm: V.tensor_copy(hfm, psT2[:]),
                 r=["psT2.0", "psT2.1"], w=[khf])
            for b, ps, kps in ((0, psQ1, "psQ1"), (1, psQ2, "psQ2")):
                for oc in range(4):
                    occ = 4 * b + oc
                    S.op("tensor", lambda ps=ps, oc=oc, occ=occ: TE.matmul(
                        ps[:, 128 * oc:128 * oc + 128], bq_rows[:, 128 * occ:128 * occ + 128],
                        ones_col[:], start=True, stop=False),
                        r=["bq_rows", "ones_col"], w=[kps])
                    for kc in range(8):
                        S.op("tensor", lambda ps=ps, oc=oc, occ=occ, kc=kc, hfm=hfm: TE.matmul(
                            ps[:, 128 * oc:128 * oc + 128],
                            Wq_sb[:, D * kc + 128 * occ:D * kc + 128 * occ + 128],
                            hfm[:, 128 * kc:128 * kc + 128], start=False, stop=(kc == 7)),
                            r=["Wq_sb", khf], w=[kps])
            S.op("tensor", lambda: TE.matmul(psS[0:64, 0:128], bdown_row[:],
                 ones_col[:], start=True, stop=False), r=["bdown_row", "ones_col"], w=["psS"])
            for kc in range(8):
                S.op("tensor", lambda kc=kc, hfm=hfm: TE.matmul(psS[0:64, 0:128],
                     Wdown_sb[:, 64 * kc:64 * kc + 64], hfm[:, 128 * kc:128 * kc + 128],
                     start=False, stop=(kc == 7)), r=["Wdown_sb", khf], w=["psS"])
            S.op("scalar", lambda: SC.copy(dumpB[0:64, 0:128], psS[0:64, 0:128]),
                 r=["psS"], w=["dumpB"])
            if dbg_on and s == 0:
                dbg_store("d0", dumpB[:, 0:128], ["dumpB"])
            kvv = k_strip[:].rearrange("p (t h) -> p h t", t=128, h=16)
            vvv = v_strip[:].rearrange("p (t h) -> p h t", t=128, h=16)
            for grp in range(4):
                ps, kps = (psK1, "psK1") if grp % 2 == 0 else (psK2, "psK2")
                for j in range(4):
                    h = 4 * grp + j
                    S.op("tensor", lambda ps=ps, j=j, h=h: TE.matmul(
                        ps[:, 128 * j:128 * j + 128], bupk_rows[:, 128 * h:128 * h + 128],
                        ones_col[:], start=True, stop=False), r=["bupk", "ones_col"], w=[kps])
                    S.op("tensor", lambda ps=ps, j=j, h=h: TE.matmul(
                        ps[:, 128 * j:128 * j + 128], Wup_sb[:, 128 * h:128 * h + 128],
                        dumpB[:, 0:128], start=False, stop=True),
                        r=["Wup_sb", "dumpB"], w=[kps])
                src = ps[:].rearrange("p (c t) -> p c t", c=4, t=128)
                for j in range(4):
                    S.op("vector", lambda src=src, grp=grp, j=j, kvv=kvv: V.tensor_copy(
                        kvv[:, 4 * grp + j, :], src[:, j, :]), r=[kps], w=["k_strip"])
            for grp in range(4):
                ps, kps = (psK1, "psK1") if grp % 2 == 0 else (psK2, "psK2")
                for j in range(4):
                    h = 4 * grp + j
                    S.op("tensor", lambda ps=ps, j=j, h=h: TE.matmul(
                        ps[:, 128 * j:128 * j + 128], bupv_rows[:, 128 * h:128 * h + 128],
                        ones_col[:], start=True, stop=False), r=["bupv", "ones_col"], w=[kps])
                    S.op("tensor", lambda ps=ps, j=j, h=h: TE.matmul(
                        ps[:, 128 * j:128 * j + 128], Wup_vpad[:, 128 * h:128 * h + 128],
                        dumpB[:, 0:128], start=False, stop=True),
                        r=["Wup_vpad", "dumpB"], w=[kps])
                src = ps[64:128, :].rearrange("p (c t) -> p c t", c=4, t=128)
                for j in range(4):
                    S.op("vector", lambda src=src, grp=grp, j=j, vvv=vvv: V.tensor_copy(
                        vvv[64:128, 4 * grp + j, :], src[:, j, :]), r=[kps], w=["v_strip"])
            qv = q_strip[:].rearrange("p (t h) -> p h t", t=128, h=16)
            for b, ps, kps in ((0, psQ1, "psQ1"), (1, psQ2, "psQ2")):
                src = ps[:].rearrange("p (c t) -> p c t", c=4, t=128)
                for oc in range(4):
                    occ = 4 * b + oc
                    S.op("scalar", lambda src=src, oc=oc, occ=occ, qv=qv: SC.copy(
                        qv[0:64, 2 * occ, :], src[0:64, oc, :]), r=[kps], w=["q_strip"])
                    S.op("scalar", lambda src=src, oc=oc, occ=occ, qv=qv: SC.copy(
                        qv[64:128, 2 * occ + 1, :], src[64:128, oc, :]), r=[kps], w=["q_strip"])
            if dbg_on and s == 0:
                dbg_store("q", q_strip[:], ["q_strip"])
                dbg_store("k", k_strip[:], ["k_strip"])
                dbg_store("v", v_strip[:], ["v_strip", "v_strip_z"])
            def blk_env(blk):
                bp = blk % 2
                return dict(
                    bp=bp,
                    pS=(psS if bp == 0 else psK1), kS=("psS" if bp == 0 else "psK1"),
                    pO=(psO if bp == 0 else psK2), kO=("psO" if bp == 0 else "psK2"),
                    Ebuf=(dumpA[:, 512:1024] if bp == 0 else dumpB[:, 512:1024]),
                    kE=("dumpA" if bp == 0 else "dumpB"),
                    Pbuf=pv_buf[:, 1024 * bp:1024 * bp + 512],
                    Vbuf=pv_buf[:, 1024 * bp + 512:1024 * bp + 1024],
                    kPV="pv%d" % bp, tcol=512 * bp,
                    kT1=f"psT1.{bp}", kT2=f"psT2.{bp}")

            def blk_A(blk):
                E = blk_env(blk)
                pS, kS, Ebuf, kE, Pbuf, kPV = E["pS"], E["kS"], E["Ebuf"], E["kE"], E["Pbuf"], E["kPV"]
                S.op("tensor", lambda pS=pS: TE.matmul(pS[:], m9l[:], m9r[:],
                     start=True, stop=False), r=["m9"], w=[kS])
                for j in range(4):
                    g = 4 * blk + j
                    S.op("tensor", lambda pS=pS, g=g, j=j: TE.matmul(
                        pS[:, 128 * j:128 * j + 128], q_strip[:, 128 * g:128 * g + 128],
                        k_strip[:, 128 * g:128 * g + 128], start=False, stop=True),
                        r=["q_strip", "k_strip"], w=[kS])
                S.op("vector", lambda pS=pS, mxc=mxc: V.tensor_reduce(
                    stats[:, mxc:mxc + 4], pS[:].rearrange("p (j t) -> p j t", j=4),
                    AX.X, ALU.max, negate=True), r=[kS], w=[stk(mxc)])
                S.op("scalar", lambda mxc=mxc: SC.activation(
                    stats[:, mxc:mxc + 4], stats[:, mxc:mxc + 4], AF.Identity, scale=0.125),
                    r=[stk(mxc)], w=[stk(mxc)])
                for j in range(4):
                    S.op("scalar", lambda pS=pS, Ebuf=Ebuf, j=j, mxc=mxc, z4c=z4c: SC.activation(
                        Ebuf[:, 128 * j:128 * j + 128], pS[:, 128 * j:128 * j + 128],
                        AF.Exp, scale=0.125, bias=st(mxc + j), accum_out=st(z4c + j)),
                        r=[kS, stk(mxc)], w=[kE, stk(z4c)])
                S.op("vector", lambda rzc=rzc, z4c=z4c: V.reciprocal(stats[:, rzc:rzc + 4],
                     stats[:, z4c:z4c + 4]), r=[stk(z4c)], w=[stk(rzc)])
                for j in range(4):
                    S.op("scalar", lambda Pbuf=Pbuf, Ebuf=Ebuf, j=j, rzc=rzc, kE=kE, kPV=kPV: SC.activation(
                        Pbuf[:, 128 * j:128 * j + 128], Ebuf[:, 128 * j:128 * j + 128],
                        AF.Identity, scale=st(rzc + j)), r=[kE, stk(rzc)], w=[kPV])

            def blk_B(blk):
                E = blk_env(blk)
                pO, kO, Pbuf, Vbuf, kPV, tcol, kT1, kT2 = (E["pO"], E["kO"], E["Pbuf"],
                    E["Vbuf"], E["kPV"], E["tcol"], E["kT1"], E["kT2"])
                for j in range(4):
                    g = 4 * blk + j
                    S.op("tensor", lambda Pbuf=Pbuf, j=j, tcol=tcol, kPV=kPV, kT1=kT1: TE.transpose(
                        psT1[:, tcol + 128 * j:tcol + 128 * j + 128],
                        Pbuf[:, 128 * j:128 * j + 128], ident[:]), r=[kPV, "ident"], w=[kT1])
                    S.op("tensor", lambda g=g, j=j, tcol=tcol, kT2=kT2: TE.transpose(
                        psT2[:, tcol + 128 * j:tcol + 128 * j + 128],
                        v_strip[:, 128 * g:128 * g + 128], ident[:]),
                        r=["v_strip", "v_strip_z", "ident"], w=[kT2])
                S.op("vector", lambda Pbuf=Pbuf, tcol=tcol, kT1=kT1, kPV=kPV: V.tensor_copy(
                     Pbuf[:], psT1[:, tcol:tcol + 512]), r=[kT1], w=[kPV])
                S.op("vector", lambda Vbuf=Vbuf, tcol=tcol, kT2=kT2, kPV=kPV: V.tensor_copy(
                     Vbuf[:], psT2[:, tcol:tcol + 512]), r=[kT2], w=[kPV])
                for j in range(4):
                    S.op("tensor", lambda pO=pO, Pbuf=Pbuf, Vbuf=Vbuf, j=j, kPV=kPV, kO=kO: TE.matmul(
                        pO[:, 128 * j:128 * j + 128], Vbuf[:, 128 * j:128 * j + 128],
                        Pbuf[:, 128 * j:128 * j + 128], start=True, stop=True),
                        r=[kPV], w=[kO])
                S.op("scalar", lambda pO=pO, blk=blk, kO=kO: SC.copy(
                    outSraw[:, 512 * blk:512 * blk + 512], pO[:]), r=[kO], w=["outSraw"])

            blk_A(0); blk_A(1); blk_B(0); blk_A(2); blk_B(1); blk_A(3); blk_B(2); blk_B(3)
            if dbg_on and s == 0:
                dbg_store("outs", outSraw[:], ["outSraw"])
            osv = outSraw[:].rearrange("p (t h) -> p h t", t=128, h=16)
            for pc in range(8):
                pp, kpp = (psS, "psS") if pc < 4 else (psO, "psO")
                col = 128 * (pc % 4)
                S.op("tensor", lambda pp=pp, pc=pc, col=col, osv=osv: TE.matmul(
                    pp[:, col:col + 128], shiftdn[:],
                    osv[:, 2 * pc, :], start=True, stop=False),
                    r=["outSraw", "shiftdn"], w=[kpp])
                S.op("tensor", lambda pp=pp, pc=pc, col=col, osv=osv: TE.matmul(
                    pp[:, col:col + 128], identbot[:],
                    osv[:, 2 * pc + 1, :], start=False, stop=True),
                    r=["outSraw", "identbot"], w=[kpp])
            S.op("vector", lambda: V.tensor_copy(ofmp[:, 0:512], psS[:]), r=["psS"], w=["ofmp"])
            S.op("vector", lambda: V.tensor_copy(ofmp[:, 512:1024], psO[:]), r=["psO"], w=["ofmp"])
            if dbg_on and s == 0:
                dbg_store("ofmp", ofmp[:], ["ofmp"])
            for fh, ps, kps in ((0, psQ1, "psQ1"), (1, psQ2, "psQ2")):
                S.op("tensor", lambda ps=ps, fh=fh: TE.matmul(
                    ps[:], ones_col[:], bo_hi[:, 512 * fh:512 * fh + 512],
                    start=True, stop=False), r=["bo_hi", "ones_col"], w=[kps])
                S.op("tensor", lambda ps=ps, fh=fh: TE.matmul(
                    ps[:], ones_col[:], bo_lo[:, 512 * fh:512 * fh + 512],
                    start=False, stop=False), r=["bo_lo", "ones_col"], w=[kps])
                for kc in range(8):
                    S.op("tensor", lambda ps=ps, fh=fh, kc=kc: TE.matmul(
                        ps[:], ofmp[:, 128 * kc:128 * kc + 128],
                        Wo_sb[:, D * kc + 512 * fh:D * kc + 512 * fh + 512],
                        start=False, stop=(kc == 7)), r=["ofmp", "Wo_sb"], w=[kps])
                S.op("vector", lambda ps=ps, us=us, fh=fh: V.scalar_tensor_tensor(
                    dumpB[:, 512 * fh:512 * fh + 512], ps[:], 1.0,
                    us[:, 512 * fh:512 * fh + 512], ALU.mult, ALU.add),
                    r=[kps, ku], w=["dumpB"])
                S.op("vector", lambda ps=ps, xs=xs, fh=fh: V.scalar_tensor_tensor(
                    xs[:, 512 * fh:512 * fh + 512], ps[:], 0.5,
                    xs[:, 512 * fh:512 * fh + 512], ALU.mult, ALU.add),
                    r=[kps, kx], w=[kx])
            if dbg_on and s == 0:
                dbg_store("z", dumpB[:], ["dumpB"])
            S.op("vector", lambda xs=xs, us=us: V.scalar_tensor_tensor(
                xs, us, 0.5, xs, ALU.mult, ALU.add), r=[ku, kx], w=[kx])
            emit_rms("dumpB", dumpB[:], r3c)
            S.op("scalar", lambda r3hc=r3hc, r3c=r3c: SC.activation(st(r3hc), st(r3c),
                 AF.Identity, scale=0.5), r=[stk(r3c)], w=[stk(r3hc)])
            S.op("vector", lambda: V.tensor_tensor(dumpB[:], dumpB[:], g2_m[:], ALU.mult),
                 r=["dumpB", "g2_m"], w=["dumpB"])
            zslab = zfm[:, D * (s % 8):D * (s % 8) + D]
            kzf = f"zfm{s % 8}"
            for c in range(8):
                S.op("tensor", lambda c=c: TE.transpose(
                    psT2[:, 128 * c:128 * c + 128], dumpB[:, 128 * c:128 * c + 128], ident[:]),
                    r=["dumpB", "ident"], w=["psT2.0", "psT2.1"])
            S.op("vector", lambda zslab=zslab: V.tensor_copy(zslab, psT2[:]),
                 r=["psT2.0", "psT2.1"], w=[kzf])
            if feed is not None:
                feed()
            S.op("tensor", lambda: TE.matmul(psO[:, 0:8], ones_col[:], bg_hi[:],
                 start=True, stop=False), r=["bg_hi", "ones_col"], w=["psO"])
            S.op("tensor", lambda: TE.matmul(psO[:, 0:8], ones_col[:], bg_lo[:],
                 start=False, stop=False), r=["bg_lo", "ones_col"], w=["psO"])
            for kc in range(8):
                S.op("tensor", lambda zslab=zslab, kc=kc, kzf=kzf: TE.matmul(
                    psO[:, 0:8], zslab[:, 128 * kc:128 * kc + 128],
                    Wg_sb[:, 8 * kc:8 * kc + 8], start=False, stop=(kc == 7)),
                    r=[kzf, "Wg_sb"], w=["psO"])
            S.op("scalar", lambda r3c=r3c: SC.activation(glog[:], psO[:, 0:8], AF.Identity,
                 scale=st(r3c)), r=["psO", stk(r3c)], w=["glog"])
            S.op("vector", lambda gtc=gtc: V.tensor_reduce(st(gtc), glog[:], AX.X, ALU.max,
                 negate=True), r=["glog"], w=[stk(gtc)])
            S.op("scalar", lambda gtc=gtc, zgc=zgc: SC.activation(glog[:], glog[:], AF.Exp,
                 bias=st(gtc), accum_out=st(zgc)), r=["glog", stk(gtc)], w=["glog", stk(zgc)])
            S.op("vector", lambda zgc=zgc: V.reciprocal(st(zgc), st(zgc)),
                 r=[stk(zgc)], w=[stk(zgc)])
            sgw = gw[:, 8 * (s % 8):8 * (s % 8) + 8]
            S.op("vector", lambda sgw=sgw, zgc=zgc, r3hc=r3hc: V.tensor_scalar(sgw, glog[:],
                 st(zgc), st(r3hc), ALU.mult, ALU.mult),
                 r=["glog", stk(zgc), stk(r3hc)], w=[f"gw{s % 8}"])

        def emit_expert_unit(e, fh, slabs):
            slot = (2 * e + fh) % 2
            wslot = We_sb[:, 4096 * slot:4096 * slot + 4096]
            kws = f"we{slot}"
            wsrc = dWe[e].rearrange("(c p) f -> p c f", p=128)[:, :, 512 * fh:512 * fh + 512]
            S.dma("gpsimd", lambda wslot=wslot, wsrc=wsrc: G.dma_start(
                wslot.rearrange("p (c f) -> p c f", c=8), wsrc), w=[kws])
            for s in slabs:
                ps, kps = (psK1, "psK1") if (s % 2 == 0) else (psK2, "psK2")
                zslab = zfm[:, D * (s % 8):D * (s % 8) + D]
                for kc in range(8):
                    S.op("tensor", lambda ps=ps, zslab=zslab, wslot=wslot, kc=kc, s=s: TE.matmul(
                        ps[:], zslab[:, 128 * kc:128 * kc + 128],
                        wslot[:, 512 * kc:512 * kc + 512],
                        start=(kc == 0), stop=(kc == 7)),
                        r=[f"zfm{s % 8}", kws], w=[kps])
                xsl = x_res[:, D * s + 512 * fh:D * s + 512 * fh + 512]
                S.op("vector", lambda ps=ps, xsl=xsl, s=s, e=e: V.scalar_tensor_tensor(
                    xsl, ps[:], gw[:, 8 * (s % 8) + e:8 * (s % 8) + e + 1], xsl,
                    ALU.mult, ALU.add), r=[kps, f"gw{s % 8}"], w=[f"x{s}"])

        def emit_ffn(fid, dbg_on):
            if fid == 1:
                S.op("vector", lambda: V.memset(q_strip[:], 0.0), w=["q_strip"])
                S.op("vector", lambda: V.memset(v_strip[0:64, :], 0.0), w=["v_strip_z", "v_strip"])
            mkfeed = lambda s: (lambda: emit_slab_head(s + 2, fid, dbg_on)) if s + 2 < NS else None
            emit_slab_head(0, fid, dbg_on)
            emit_slab_head(1, fid, dbg_on)
            for s in range(8):
                emit_slab_body(s, fid, dbg_on, feed=mkfeed(s))
            sl0 = list(range(8))
            for e in range(8):
                for fh in range(2):
                    emit_expert_unit(e, fh, sl0)
            for s in range(8, 16):
                emit_slab_body(s, fid, dbg_on, feed=mkfeed(s))
            if dbg_on:
                dbg_store("gate", gw[:], [f"gw{i}" for i in range(8)])
                dbg_store("zfm", zfm[:], [f"zfm{i}" for i in range(8)])
            sl1 = list(range(8, 16))
            return [(lambda e=e, fh=fh: emit_expert_unit(e, fh, sl1))
                    for e in range(8) for fh in range(2)]

        def emit_attn_part1(dbg_on, feeder=()):
            S.dma("gpsimd", lambda: G.dma_start(We_sb[:].rearrange("p (c f) -> p c f", c=8), dWr.rearrange("(c p) f -> p c f", p=128)),
                  w=["we0", "we1"])
            for c in range(8):
                for oc in range(8):
                    S.op("tensor", lambda c=c, oc=oc: TE.transpose(
                        psT1[:, 0:128], We_sb[:, D * oc + 128 * c:D * oc + 128 * c + 128], ident[:]),
                        r=["we0", "we1", "ident"], w=["psT1.0", "psT1.1"])
                    S.op("vector", lambda c=c, oc=oc: V.tensor_copy(
                        Wq_sb[:, D * c + 128 * oc:D * c + 128 * oc + 128], psT1[:, 0:128]),
                        r=["psT1.0", "psT1.1"], w=["Wq_sb"])
            S.dma("gpsimd", lambda: G.dma_start(We_sb[:].rearrange("p (c f) -> p c f", c=8), dWf.rearrange("(c p) f -> p c f", p=128)),
                  w=["we0", "we1"])
            for c in range(8):
                for oc in range(8):
                    S.op("tensor", lambda c=c, oc=oc: TE.transpose(
                        psT1[:, 0:128], We_sb[:, D * oc + 128 * c:D * oc + 128 * c + 128], ident[:]),
                        r=["we0", "we1", "ident"], w=["psT1.0", "psT1.1"])
                    S.op("vector", lambda c=c, oc=oc: V.tensor_copy(
                        Wo_sb[:, D * c + 128 * oc:D * c + 128 * oc + 128], psT1[:, 0:128]),
                        r=["psT1.0", "psT1.1"], w=["Wo_sb"])
            bcast_rows(lng_m, "q_strip", dlng)
            bcast_rows(lnb_m, "k_strip", dlnb)
            S.dma("sync", lambda: SY.dma_start(row_f[:, 0:128], dattr), w=["row_f"])
            S.op("vector", lambda: V.tensor_copy(row_hi[:, 0:128], row_f[:, 0:128]),
                 r=["row_f"], w=["row_hi"])
            for rep in range(8):
                S.op("tensor", lambda rep=rep: TE.matmul(psS[:, 0:128], ones_col[:],
                     row_hi[:, 0:128], start=True, stop=True), r=["row_hi", "ones_col"], w=["psS"])
                S.op("vector", lambda rep=rep: V.tensor_copy(
                    attr_m[:, 128 * rep:128 * rep + 128], psS[:, 0:128]), r=["psS"], w=["outSraw"])
            def a1_head(s):
                xs = x_res[:, D * s:D * s + D]
                us = u_ring[:, D * (s % 2):D * (s % 2) + D]
                kx, ku = f"x{s}", f"u{s % 2}"
                emit_ln_u(s, kx, xs, ku, us)
                if dbg_on and s == 0:
                    dbg_store("u2", u_ring[:, 0:D], [ku])
                hs = h_tok[:, D * (s % 2):D * (s % 2) + D]
                S.op("scalar", lambda us=us, hs=hs, s=s: SC.copy(hs, us),
                     r=[ku], w=[f"h{s % 2}"])

            a1_head(0)
            for s in range(NS):
                hs = h_tok[:, D * (s % 2):D * (s % 2) + D]
                kh = f"h{s % 2}"
                hfm = h_fm[:, D * (s % 2):D * (s % 2) + D]
                khf = f"hfm{s % 2}"
                for c in range(8):
                    S.op("tensor", lambda hs=hs, c=c: TE.transpose(
                        psT2[:, 128 * c:128 * c + 128], hs[:, 128 * c:128 * c + 128], ident[:]),
                        r=[kh, "ident"], w=["psT2.0", "psT2.1"])
                S.op("vector", lambda hfm=hfm: V.tensor_copy(hfm, psT2[:]), r=["psT2.0", "psT2.1"], w=[khf])
                for fh, ps, kps in ((0, psQ1, "psQ1"), (1, psQ2, "psQ2")):
                    for kc in range(8):
                        S.op("tensor", lambda ps=ps, hfm=hfm, fh=fh, kc=kc: TE.matmul(
                            ps[:], hfm[:, 128 * kc:128 * kc + 128],
                            Wq_sb[:, D * kc + 512 * fh:D * kc + 512 * fh + 512],
                            start=(kc == 0), stop=(kc == 7)), r=[khf, "Wq_sb"], w=[kps])
                S.op("scalar", lambda: SC.copy(ofmp[:, 0:512], psQ1[:]), r=["psQ1"], w=["ofmp"])
                S.op("scalar", lambda: SC.copy(ofmp[:, 512:1024], psQ2[:]), r=["psQ2"], w=["ofmp"])
                S.dma("sync", lambda s=s: SY.dma_start(FRS[:, 1024 * s:1024 * s + 1024], ofmp[:]),
                      r=["ofmp"])
                S.op("scalar", lambda: SC.activation(dumpA[:, 0:512], psQ1[:], AF.Lrelu, alpha=0.01),
                     r=["psQ1"], w=["dumpA"])
                S.op("scalar", lambda: SC.activation(dumpA[:, 512:1024], psQ2[:], AF.Lrelu, alpha=0.01),
                     r=["psQ2"], w=["dumpA"])
                S.op("vector", lambda: V.tensor_tensor(dumpB[:], dumpA[:], attr_m[:], ALU.mult),
                     r=["dumpA", "outSraw"], w=["dumpB"])
                S.op("vector", lambda: V.tensor_reduce(srtok_f[:], dumpB[:].rearrange(
                     "p (h d) -> p h d", h=8), AX.X, ALU.add), r=["dumpB"], w=["srtok_f"])
                S.op("scalar", lambda: SC.copy(srtok_bf[:], srtok_f[:]),
                     r=["srtok_f"], w=["srtok_bf"])
                if s < 8:
                    for u in feeder[2 * s:2 * s + 2]:
                        u()
                    a1_head(s + 1)
                    if s == 7:
                        a1_head(9)
                elif s + 2 < NS:
                    a1_head(s + 2)
                S.op("tensor", lambda: TE.transpose(psT1[0:8, 0:128], srtok_bf[:], ident[:]),
                     r=["srtok_bf", "ident"], w=["psT1.0", "psT1.1"])
                S.op("vector", lambda s=s: V.tensor_copy(sr_all[:, 128 * s:128 * s + 128],
                     psT1[0:8, 0:128]), r=["psT1.0", "psT1.1"], w=[f"sr{s}", "pv0", "pv1"])
            srk = [f"sr{s}" for s in range(NS)]
            if dbg_on:
                dbg_store("sr", sr_all[:], srk)
            S.op("vector", lambda: V.tensor_reduce(stats[0:8, c_pm:c_pm + 1], sr_all[:], AX.X,
                 ALU.max, negate=True), r=srk + ["pv0", "pv1"], w=[stk(c_pm)])
            S.op("scalar", lambda: SC.activation(sr_all[:], sr_all[:], AF.Exp,
                 bias=stats[0:8, c_pm:c_pm + 1], accum_out=stats[0:8, c_pz:c_pz + 1]),
                 r=[stk(c_pm)] + srk, w=[stk(c_pz)] + srk + ["pv0", "pv1"])
            S.op("vector", lambda: V.reciprocal(stats[0:8, c_pz:c_pz + 1],
                 stats[0:8, c_pz:c_pz + 1]), r=[stk(c_pz)], w=[stk(c_pz)])
            S.op("scalar", lambda: SC.activation(sr_all[:], sr_all[:], AF.Identity,
                 scale=stats[0:8, c_pz:c_pz + 1]), r=[stk(c_pz)] + srk, w=srk + ["pv0", "pv1"])
            if dbg_on:
                dbg_store("p", sr_all[:], srk)
            for s in range(NS):
                S.op("tensor", lambda s=s: TE.transpose(psT2[:, 8 * s:8 * s + 8],
                     sr_all[0:8, 128 * s:128 * s + 128], ident[0:8, 0:8]), r=[f"sr{s}", "pv0", "pv1", "ident"], w=["psT2.0", "psT2.1"])
            S.op("vector", lambda: V.tensor_copy(pT[:], psT2[:, 0:128]), r=["psT2.0", "psT2.1"], w=["pT"])
            for s in range(NS):
                slot = s % 2
                fbuf = h_fm[:, D * slot:D * slot + D]
                kf = f"hfm{slot}"
                S.dma("sync", lambda s=s, fbuf=fbuf: SY.dma_start(fbuf, FRS[:, 1024 * s:1024 * s + 1024]),
                      w=[kf])
                for h in range(8):
                    S.op("tensor", lambda h=h, s=s, fbuf=fbuf, kf=kf: TE.matmul(
                        psK1[:, h:h + 1], fbuf[:, 128 * h:128 * h + 128],
                        pT[:, 8 * s + h:8 * s + h + 1], start=(s == 0 and h == 0),
                        stop=(s == NS - 1)), r=[kf, "pT"], w=["psK1"])
            S.op("vector", lambda: V.tensor_copy(ctx_bf[:], psK1[:, 0:8]), r=["psK1"], w=["ctx_bf"])
            if dbg_on:
                dbg_store("ctx", ctx_bf[:], ["ctx_bf"])
            for oc in range(8):
                for c in range(8):
                    S.op("tensor", lambda oc=oc, c=c: TE.matmul(
                        psK2[:, oc:oc + 1], Wo_sb[:, D * c + 128 * oc:D * c + 128 * oc + 128],
                        ctx_bf[:, c:c + 1], start=(c == 0), stop=(c == 7)),
                        r=["Wo_sb", "ctx_bf"], w=["psK2"])
            S.op("vector", lambda: V.tensor_copy(fhFM_bf[:], psK2[:, 0:8]), r=["psK2"], w=["fhFM_bf"])
            S.op("tensor", lambda: TE.transpose(psT1[0:8, 0:128], fhFM_bf[:], ident[:]),
                 r=["fhFM_bf", "ident"], w=["psT1.0", "psT1.1"])
            S.op("vector", lambda: V.tensor_copy(fhT8[:], psT1[0:8, 0:128]), r=["psT1.0", "psT1.1"], w=["fhT8"])
            for c in range(8):
                S.dma("gpsimd", lambda c=c: G.dma_start(row_hi[:, 128 * c:128 * c + 128],
                      fhT8[c:c + 1, :]), r=["fhT8"], w=["row_hi"])
            for hh in range(0, D, 512):
                S.op("tensor", lambda hh=hh: TE.matmul(psS[:], ones_col[:],
                     row_hi[:, hh:hh + 512], start=True, stop=True),
                     r=["row_hi", "ones_col"], w=["psS"])
                S.op("vector", lambda hh=hh: V.tensor_copy(fh_m[:, hh:hh + 512], psS[:]),
                     r=["psS"], w=["outSraw"])
            if dbg_on:
                dbg_store("fh", fh_m[:], ["outSraw"])

        def emit_attn_part2(dbg_on):
            def p2_head(s):
                xs = x_res[:, D * s:D * s + D]
                us = u_ring[:, D * (s % 2):D * (s % 2) + D]
                emit_ln_u(s, f"x{s}", xs, f"u{s % 2}", us)

            p2_head(0)
            for s in range(NS):
                xs = x_res[:, D * s:D * s + D]
                uslot = s % 2
                sp = s % 2
                us = u_ring[:, D * uslot:D * uslot + D]
                kx, ku = f"x{s}", f"u{uslot}"
                lsum, lmu, lv, lt, lr = (cL_sum[sp], cL_mu[sp], cL_v[sp], cL_t[sp], cL_r[sp])
                ymuc, yrc = c_ymu[sp], c_yr[sp]
                if s + 1 < NS:
                    p2_head(s + 1)
                S.op("vector", lambda us=us: V.tensor_tensor(us, us, fh_m[:], ALU.add),
                     r=[ku, "outSraw"], w=[ku])
                S.op("vector", lambda us=us, lsum=lsum: V.tensor_reduce(st(lsum), us, AX.X, ALU.add),
                     r=[ku], w=[stk(lsum)])
                S.op("scalar", lambda us=us, lv=lv: SC.activation(dumpA[:], us, AF.Square,
                     accum_out=st(lv)), r=[ku], w=["dumpA", stk(lv)])
                S.op("scalar", lambda ymuc=ymuc, lsum=lsum: SC.activation(st(ymuc), st(lsum),
                     AF.Identity, scale=1.0 / D), r=[stk(lsum)], w=[stk(ymuc)])
                S.op("scalar", lambda lt=lt, ymuc=ymuc: SC.activation(st(lt), st(ymuc), AF.Square),
                     r=[stk(ymuc)], w=[stk(lt)])
                S.op("vector", lambda lv=lv, lt=lt: V.tensor_scalar(st(lv), st(lv), 1.0 / D, st(lt),
                     ALU.mult, ALU.subtract), r=[stk(lv), stk(lt)], w=[stk(lv)])
                S.op("scalar", lambda lv=lv: SC.activation(st(lv), st(lv), AF.Ln, bias=eps_ln[:]),
                     r=[stk(lv), "eps_ln"], w=[stk(lv)])
                S.op("scalar", lambda yrc=yrc, lv=lv: SC.activation(st(yrc), st(lv), AF.Exp, scale=-0.5),
                     r=[stk(lv)], w=[stk(yrc)])
                S.op("vector", lambda us=us, ymuc=ymuc, yrc=yrc: V.tensor_scalar(us, us, st(ymuc), st(yrc),
                     ALU.subtract, ALU.mult), r=[ku, stk(ymuc), stk(yrc)], w=[ku])
                S.op("gpsimd", lambda us=us: G.tensor_tensor(us, us, lng_m[:], ALU.mult),
                     r=[ku, "q_strip"], w=[ku])
                S.op("gpsimd", lambda us=us: G.tensor_tensor(us, us, lnb_m[:], ALU.add),
                     r=[ku, "k_strip"], w=[ku])
                S.op("vector", lambda xs=xs, us=us: V.tensor_tensor(xs, xs, us, ALU.add),
                     r=[ku, kx], w=[kx])
            if dbg_on:
                dbg_store("x2", x_res[:, 0:D], ["x0"])

        DBG = bool(debug)
        load_ffn_weights()
        dbg_store("wup", Wup_sb[:], ["Wup_sb"])
        dbg_store("wdn", Wdown_sb[:], ["Wdown_sb"])
        dbg_store("wq", Wq_sb[:], ["Wq_sb"])
        dbg_store("bupk", bupk_rows[:], ["bupk"])
        dbg_store("bupv", bupv_rows[:], ["bupv"])
        dbg_store("wvp", Wup_vpad[:], ["Wup_vpad"])
        pend1 = emit_ffn(0, DBG)
        if DBG:
            dbg_store("x1", x_res[:, 0:D], ["x0"])
        emit_attn_part1(DBG, feeder=pend1)
        load_ffn_weights()
        emit_attn_part2(DBG)
        pend2 = emit_ffn(1, False)
        for u in pend2:
            u()

        for s in range(NS):
            src = x_res[:, D * s:D * s + D]
            S.dma("sync", lambda s=s, src=src: SY.dma_start(OUT[128 * s:128 * s + 128, :], src),
                  r=[f"x{s}"])

        S.finalize(nc, ctx)
        block = ctx.enter_context(nc.Block())

        @block.tensor
        def _(te):
            S.emit_engine("tensor", te)

        @block.vector
        def _(ve):
            S.emit_engine("vector", ve)

        @block.scalar
        def _(se):
            S.emit_engine("scalar", se)

        @block.gpsimd
        def _(ge):
            S.emit_engine("gpsimd", ge)
            S.final_wait("gpsimd", ge)

        @block.sync
        def _(sy):
            S.emit_engine("sync", sy)
            S.final_wait("sync", sy)

    return nc


_NC_CACHE = {}


def _get_nc(debug=()):
    key = tuple(debug)
    if key not in _NC_CACHE:
        _NC_CACHE[key] = build_nc(debug)
    return _NC_CACHE[key]


def make_in_maps(inputs):
    x = np.ascontiguousarray(np.asarray(inputs["L0_attn_h"], dtype=np.float32))
    common = dict(
        Wq=np.ascontiguousarray(inputs["L0_ffn_Wq"], np.float32),
        bq=np.ascontiguousarray(np.asarray(inputs["L0_ffn_bq"], np.float32).reshape(1, -1)),
        Wdown=np.ascontiguousarray(inputs["L0_ffn_Wdown"], np.float32),
        bdown=np.ascontiguousarray(np.asarray(inputs["L0_ffn_bdown"], np.float32).reshape(1, -1)),
        Wup=np.ascontiguousarray(inputs["L0_ffn_Wup"], np.float32),
        bup=np.ascontiguousarray(np.asarray(inputs["L0_ffn_bup"], np.float32).reshape(1, -1)),
        Wo=np.ascontiguousarray(inputs["L0_ffn_Wo"], np.float32),
        bo=np.ascontiguousarray(np.asarray(inputs["L0_ffn_bo"], np.float32).reshape(1, -1)),
        Wg=np.ascontiguousarray(inputs["L0_ffn_Wg"], np.float32),
        bg=np.ascontiguousarray(np.asarray(inputs["L0_ffn_bg"], np.float32).reshape(1, -1)),
        We=np.ascontiguousarray(inputs["L0_ffn_We"], np.float32),
        gamma1=np.ascontiguousarray(np.asarray(inputs["L0_ffn_gamma1"], np.float32).reshape(1, -1)),
        gamma2=np.ascontiguousarray(np.asarray(inputs["L0_ffn_gamma2"], np.float32).reshape(1, -1)),
        Wr=np.ascontiguousarray(inputs["L0_attn_Wr"], np.float32),
        Wf=np.ascontiguousarray(inputs["L0_attn_Wf"], np.float32),
        att_r=np.ascontiguousarray(np.asarray(inputs["L0_attn_att_r"], np.float32).reshape(1, -1)),
        ln_gamma=np.ascontiguousarray(np.asarray(inputs["L0_attn_ln_gamma"], np.float32).reshape(1, -1)),
        ln_beta=np.ascontiguousarray(np.asarray(inputs["L0_attn_ln_beta"], np.float32).reshape(1, -1)),
    )
    return [dict(common, x=np.ascontiguousarray(x[b])) for b in range(x.shape[0])]


def kernel(**inputs):
    in_maps = make_in_maps(inputs)
    nc = _get_nc()
    res = run_bass_kernel_spmd(nc, in_maps, core_ids=list(range(8)))
    return np.stack([r["out"] for r in res.results], axis=0)
